# version 07
# baseline (speedup 1.0000x reference)
import numpy as np
import ml_dtypes
import concourse.bass as bass
import concourse.mybir as mybir
from concourse.bass_utils import run_bass_kernel_spmd

F32 = mybir.dt.float32
BF16 = mybir.dt.bfloat16
AF = mybir.ActivationFunctionType
ALU = mybir.AluOpType
AX = mybir.AxisListType

E = 2048
T = 2048
H = 16
LAT = 512
I = 8192
NB = 2
SCALE = LAT ** -0.5
EPS = 1e-5
NEG = -1e9

COMPUTE_SEMS = ("pe", "ac", "ve")


class Ctl:

    def __init__(self, nc, engines):
        self.nc = nc
        self.engines = engines
        self.sems = {}
        self.cnt = {}
        self.waited = {}
        self.free = {}

    def sem(self, name):
        self.sems[name] = self.nc.alloc_semaphore(name)
        self.cnt[name] = 0

    def inc(self, instr, s, n=1):
        instr.then_inc(self.sems[s], n)
        self.cnt[s] += n
        return self.cnt[s]

    def wait(self, eng, s, v):
        if v <= 0:
            return
        assert v <= self.cnt[s], (s, v, self.cnt[s])
        k = (id(eng), s)
        if self.waited.get(k, 0) >= v:
            return
        eng.wait_ge(self.sems[s], v)
        self.waited[k] = v

    def use(self, eng, res):
        cond = self.free.get(res)
        if cond is not None:
            self.wait(eng, cond[0], cond[1])

    def rel(self, res, s, v):
        self.free[res] = (s, v)

    def end_block(self):
        for eng in self.engines:
            for s in COMPUTE_SEMS:
                self.waited[(id(eng), s)] = self.cnt[s]
        self.free = {}


def build_nc():
    nc = bass.Bass("TRN2", target_bir_lowering=False)
    TEN, VEC, ACT, SYN = nc.tensor, nc.vector, nc.scalar, nc.sync

    x_d = nc.dram_tensor("x", [NB, T, E], F32, kind="ExternalInput")
    out_d = nc.dram_tensor("out", [NB, T, E], F32, kind="ExternalOutput")
    wf, wb = {}, {}
    wshapes = {
        "wq": [16, 128, 2048], "wk": [128, 8192], "wv": [128, 8192],
        "q2l": [16, 128, 512], "vup": [16, 128, 512], "wout": [16, 128, 2048],
        "wg": [64, 128, 2048], "wu": [64, 128, 2048], "wd": [16, 128, 8192],
    }
    for n, sh in wshapes.items():
        wf[n] = nc.dram_tensor(n, sh, F32, kind="ExternalInput")
        wb[n] = nc.dram_tensor(n + "b", sh, BF16, kind="Internal")
    masks_d = nc.dram_tensor("masks", [128, 2048], BF16, kind="ExternalInput")
    ident_d = nc.dram_tensor("ident", [128, 128], BF16, kind="ExternalInput")
    ones_d = nc.dram_tensor("ones", [128, 128], BF16, kind="ExternalInput")
    cvals_d = nc.dram_tensor("cvals", [128, 8], F32, kind="ExternalInput")
    yscr = nc.dram_tensor("yscr", [NB, T, E], BF16, kind="Internal")

    ln1T = nc.alloc_sbuf_tensor("ln1T", [128, 32768], BF16)
    OT = nc.alloc_sbuf_tensor("OT", [128, 8192], BF16)
    kv = nc.alloc_sbuf_tensor("kv", [128, 16384], BF16)
    wbuf = nc.alloc_sbuf_tensor("wbuf", [128, 16384], BF16)
    ybuf = nc.alloc_sbuf_tensor("ybuf", [128, 8192], BF16)
    aux = nc.alloc_sbuf_tensor("aux", [128, 8192], BF16)
    xf = nc.alloc_sbuf_tensor("xf", [128, 6144], F32)
    consts = nc.alloc_sbuf_tensor("cmask", [128, 2048], BF16)
    ident = nc.alloc_sbuf_tensor("ident_sb", [128, 128], BF16)
    ones = nc.alloc_sbuf_tensor("ones_sb", [128, 128], BF16)
    stats = nc.alloc_sbuf_tensor("stats", [128, 16], F32)
    rzf = nc.alloc_sbuf_tensor("rzf", [1, 512], F32)
    rzb = nc.alloc_sbuf_tensor("rzb", [1, 512], BF16)
    cvals = nc.alloc_sbuf_tensor("cvals_sb", [128, 8], F32)
    nc.const_aps.register(F32, EPS, cvals[:, 0:1])
    nc.const_aps.register(F32, -0.5, cvals[:, 1:2])
    nc.const_aps.register(F32, SCALE, cvals[:, 2:3])

    pa = [nc.alloc_psum_tensor(f"pa{i}", [128, 512], F32) for i in range(4)]
    ps = [nc.alloc_psum_tensor(f"ps{i}", [128, 512], F32) for i in range(2)]
    pz = nc.alloc_psum_tensor("pz", [128, 512], F32)
    pt = nc.alloc_psum_tensor("pt", [128, 1024], BF16)

    engines = [TEN, VEC, ACT, SYN]
    C = Ctl(nc, engines)
    for s in ("qd", "qs", "pe", "ac", "ve"):
        C.sem(s)

    def ddma(eng, qsem, out, in_):
        return C.inc(eng.dma_start(out=out, in_=in_), qsem, 16)

    def qs_guard():
        for eng in (SYN, ACT, VEC):
            C.wait(eng, "qs", C.cnt["qs"])

    def ln_tile(in_tile, sqdump, out_ap, pre_wait=None, out_war=None):
        if pre_wait is not None:
            C.wait(VEC, pre_wait[0], pre_wait[1])
        C.inc(VEC.scalar_tensor_tensor(
            out=sqdump, in0=in_tile, scalar=1.0, in1=in_tile,
            op0=ALU.mult, op1=ALU.mult, accum_out=stats[:, 1:2]), "ve")
        v = C.inc(VEC.reduce_sum(stats[:, 0:1], in_tile, axis=AX.X), "ve")
        C.wait(VEC, "ve", v)
        v = C.inc(VEC.tensor_scalar_mul(stats[:, 2:3], stats[:, 0:1], 1.0 / E), "ve")
        C.wait(VEC, "ve", v)
        v = C.inc(VEC.tensor_mul(stats[:, 3:4], stats[:, 2:3], stats[:, 2:3]), "ve")
        C.wait(VEC, "ve", v)
        v = C.inc(VEC.scalar_tensor_tensor(
            out=stats[:, 4:5], in0=stats[:, 1:2], scalar=1.0 / E, in1=stats[:, 3:4],
            op0=ALU.mult, op1=ALU.subtract), "ve")
        C.wait(ACT, "ve", v)
        a = C.inc(ACT.activation(stats[:, 5:6], stats[:, 4:5], AF.Ln, bias=EPS), "ac")
        C.wait(ACT, "ac", a)
        a = C.inc(ACT.activation(stats[:, 6:7], stats[:, 5:6], AF.Exp, scale=-0.5), "ac")
        C.wait(VEC, "ac", a)
        if out_war is not None:
            C.use(VEC, out_war)
        v = C.inc(VEC.tensor_scalar(
            out=out_ap, in0=in_tile, scalar1=stats[:, 2:3], scalar2=stats[:, 6:7],
            op0=ALU.subtract, op1=ALU.mult), "ve")
        return v

    def transpose_tile(src_col_fn, dest_sb, dest_col_fn, src_gate):
        last_v = None
        for g in range(4):
            C.use(TEN, "pt")
            C.wait(TEN, src_gate[0], src_gate[1])
            for j in range(4):
                ins = TEN.transpose(pt[:, j * 128:(j + 1) * 128],
                                    src_col_fn(g * 4 + j), ident[:, :])
            p = C.inc(ins, "pe")
            C.wait(VEC, "pe", p)
            for j in range(4):
                dc = dest_col_fn(g * 4 + j)
                last_v = C.inc(VEC.tensor_copy(
                    dest_sb[:, dc:dc + 128], pt[:, j * 128:(j + 1) * 128]), "ve")
            C.rel("pt", "ve", last_v)
        return last_v, C.cnt["pe"]

    with nc.Block():
        ddma(SYN, "qd", consts[:, :], masks_d[:, :])
        ddma(SYN, "qd", ident[:, :], ident_d[:, :])
        ddma(SYN, "qd", ones[:, :], ones_d[:, :])
        ddma(SYN, "qd", cvals[:, :], cvals_d[:, :])

        xin = [xf[:, 0:2048], xf[:, 2048:4096]]
        bout = [aux[:, 0:2048], aux[:, 2048:4096]]
        ci = 0
        for n, sh in wshapes.items():
            nel = int(np.prod(sh))
            nch = nel // (128 * 2048)
            if len(sh) == 3:
                src = wf[n][:, :, :].rearrange("a b c -> (a b c)")
                dst = wb[n][:, :, :].rearrange("a b c -> (a b c)")
            else:
                src = wf[n][:, :].rearrange("a b -> (a b)")
                dst = wb[n][:, :].rearrange("a b -> (a b)")
            src = src.rearrange("(n p w) -> n p w", p=128, w=2048)
            dst = dst.rearrange("(n p w) -> n p w", p=128, w=2048)
            for i in range(nch):
                sl = ci % 2
                ldq, lds = (SYN, "qd") if sl == 0 else (ACT, "qs")
                stq, sts = (ACT, "qs") if sl == 0 else (SYN, "qd")
                C.use(ldq, f"p0in{sl}")
                ld = ddma(ldq, lds, xin[sl], src[i, :, :])
                eng, esem = (ACT, "ac") if sl == 0 else (VEC, "ve")
                C.wait(eng, lds, ld)
                C.use(eng, f"p0out{sl}")
                if sl == 0:
                    cv = C.inc(ACT.copy(bout[sl], xin[sl]), "ac")
                else:
                    cv = C.inc(VEC.tensor_copy(bout[sl], xin[sl]), "ve")
                C.rel(f"p0in{sl}", esem, cv)
                C.wait(stq, esem, cv)
                st = ddma(stq, sts, dst[i, :, :], bout[sl])
                C.rel(f"p0out{sl}", sts, st)
                ci += 1
        SYN.wait_ge(C.sems["qs"], C.cnt["qs"])
        SYN.wait_ge(C.sems["qd"], C.cnt["qd"])
        C.waited[(id(SYN), "qs")] = C.cnt["qs"]
        C.waited[(id(SYN), "qd")] = C.cnt["qd"]
    C.end_block()

    for b in range(NB):
        with nc.Block():
            qs_guard()
            for tt in range(16):
                C.use(SYN, "xf0")
                ld = ddma(SYN, "qd", xf[:, 0:2048], x_d[b, tt * 128:(tt + 1) * 128, :])
                C.wait(VEC, "qd", ld)
                v = ln_tile(xf[:, 0:2048], xf[:, 4096:6144], aux[:, 0:2048],
                            out_war="auxtile")
                C.rel("xf0", "ve", v)
                _, pe_src = transpose_tile(
                    lambda ec: aux[:, ec * 128:(ec + 1) * 128],
                    ln1T, lambda ec: ec * 2048 + tt * 128, ("ve", v))
                C.rel("auxtile", "pe", pe_src)
        C.end_block()

        with nc.Block():
            qs_guard()
            lwk = ddma(SYN, "qd", wbuf[:, 0:8192], wb["wk"][:, :])
            lwv = ddma(SYN, "qd", wbuf[:, 8192:16384], wb["wv"][:, :])
            C.wait(TEN, "qd", lwk)
            rot = 0
            for lc in range(4):
                for tc in range(4):
                    bank = f"pa{rot % 4}"
                    C.use(TEN, bank)
                    for ec in range(16):
                        ins = TEN.matmul(
                            pa[rot % 4][:, :],
                            wbuf[:, ec * 512 + lc * 128: ec * 512 + lc * 128 + 128],
                            ln1T[:, ec * 2048 + tc * 512: ec * 2048 + (tc + 1) * 512],
                            start=(ec == 0), stop=(ec == 15))
                    p = C.inc(ins, "pe")
                    C.wait(ACT, "pe", p)
                    a = C.inc(ACT.copy(kv[:, lc * 2048 + tc * 512: lc * 2048 + (tc + 1) * 512],
                                       pa[rot % 4][:, :]), "ac")
                    C.rel(bank, "ac", a)
                    rot += 1
            C.wait(TEN, "qd", lwv)
            for tt in range(16):
                bank = f"pa{rot % 4}"
                C.use(TEN, bank)
                for ec in range(16):
                    ins = TEN.matmul(
                        pa[rot % 4][:, :],
                        ln1T[:, ec * 2048 + tt * 128: ec * 2048 + tt * 128 + 128],
                        wbuf[:, 8192 + ec * 512: 8192 + (ec + 1) * 512],
                        start=(ec == 0), stop=(ec == 15))
                p = C.inc(ins, "pe")
                C.wait(ACT, "pe", p)
                a = C.inc(ACT.copy(kv[:, 8192 + tt * 512: 8192 + (tt + 1) * 512],
                                   pa[rot % 4][:, :]), "ac")
                C.rel(bank, "ac", a)
                rot += 1
        C.end_block()

        with nc.Block():
            qs_guard()
            for qc in range(4):
                nkt = 4 * qc + 4
                C.wait(ACT, "qs", C.cnt["qs"])
                C.wait(VEC, "qs", C.cnt["qs"])

                def load_head(h):
                    sl = h % 2
                    C.use(SYN, f"wq{sl}")
                    ddma(SYN, "qd", wbuf[:, sl * 2048:(sl + 1) * 2048], wb["wq"][h, :, :])
                    C.use(SYN, f"q2l{sl}")
                    ddma(SYN, "qd", wbuf[:, 4096 + sl * 512: 4096 + (sl + 1) * 512],
                         wb["q2l"][h, :, :])
                    C.use(SYN, f"vup{sl}")
                    return ddma(SYN, "qd", wbuf[:, 5120 + sl * 512: 5120 + (sl + 1) * 512],
                                wb["vup"][h, :, :])

                warr = {0: load_head(0)}

                def emit_qh(h):
                    sl = h % 2
                    C.wait(TEN, "qd", warr[h])
                    C.use(TEN, "ps0")
                    for ec in range(16):
                        ins = TEN.matmul(
                            ps[0][:, :],
                            wbuf[:, sl * 2048 + ec * 128: sl * 2048 + ec * 128 + 128],
                            ln1T[:, ec * 2048 + qc * 512: ec * 2048 + (qc + 1) * 512],
                            start=(ec == 0), stop=(ec == 15))
                    p = C.inc(ins, "pe")
                    C.rel(f"wq{sl}", "pe", p)
                    return p

                p_qh = {0: emit_qh(0)}
                a_o = 0
                for h in range(H):
                    sl = h % 2
                    if h + 1 < H:
                        warr[h + 1] = load_head(h + 1)
                    q2l_sl = wbuf[:, 4096 + sl * 512: 4096 + (sl + 1) * 512]
                    vup_sl = wbuf[:, 5120 + sl * 512: 5120 + (sl + 1) * 512]
                    C.wait(ACT, "pe", p_qh[h])
                    C.use(ACT, "qh")
                    a_qh = C.inc(ACT.copy(ybuf[:, 0:512], ps[0][:, :]), "ac")
                    C.rel("ps0", "ac", a_qh)
                    a_ql = 0
                    for lc in range(4):
                        C.use(TEN, f"pa{lc}")
                        C.wait(TEN, "ac", a_qh)
                        pm = C.inc(TEN.matmul(
                            pa[lc][:, :], q2l_sl[:, lc * 128:(lc + 1) * 128],
                            ybuf[:, 0:512], start=True, stop=True), "pe")
                        if lc == 3:
                            C.rel("qh", "pe", pm)
                        C.wait(ACT, "pe", pm)
                        C.use(ACT, "qlat")
                        a_ql = C.inc(ACT.activation(
                            ybuf[:, 512 + lc * 512: 512 + (lc + 1) * 512],
                            pa[lc][:, :], AF.Copy, scale=SCALE), "ac")
                        C.rel(f"pa{lc}", "ac", a_ql)
                    p_ctx_last = {}
                    p_z = 0

                    def emit_S(kt):
                        sb = kt % 2
                        C.use(TEN, f"ps{sb}")
                        C.wait(TEN, "ac", a_ql)
                        for lc in range(4):
                            ins = TEN.matmul(
                                ps[sb][:, :],
                                kv[:, lc * 2048 + kt * 128: lc * 2048 + kt * 128 + 128],
                                ybuf[:, 512 + lc * 512: 512 + (lc + 1) * 512],
                                start=(lc == 0), stop=(lc == 3))
                        return C.inc(ins, "pe")

                    p_s = emit_S(0)
                    for kt in range(nkt):
                        sb = kt % 2
                        C.wait(ACT, "pe", p_s)
                        C.use(ACT, f"E{sb}")
                        a_e = C.inc(ACT.activation(
                            ybuf[:, 2560 + sb * 512: 2560 + (sb + 1) * 512],
                            ps[sb][:, :], AF.Exp), "ac")
                        C.rel(f"ps{sb}", "ac", a_e)
                        e_gate = ("ac", a_e)
                        C.wait(VEC, "ac", a_e)
                        if kt >= 4 * qc:
                            m = kt - 4 * qc
                            vm = C.inc(VEC.tensor_tensor(
                                ybuf[:, 2560 + sb * 512: 2560 + (sb + 1) * 512],
                                ybuf[:, 2560 + sb * 512: 2560 + (sb + 1) * 512],
                                consts[:, m * 512:(m + 1) * 512], ALU.mult), "ve")
                            e_gate = ("ve", vm)
                        if kt == 0:
                            C.inc(VEC.tensor_copy(
                                xf[:, 2048:2560],
                                ybuf[:, 2560 + sb * 512: 2560 + (sb + 1) * 512]), "ve")
                        else:
                            C.inc(VEC.tensor_add(
                                xf[:, 2048:2560], xf[:, 2048:2560],
                                ybuf[:, 2560 + sb * 512: 2560 + (sb + 1) * 512]), "ve")
                        if kt + 1 < nkt:
                            p_s = emit_S(kt + 1)
                        C.wait(TEN, e_gate[0], e_gate[1])
                        for lc in range(4):
                            if kt == 0:
                                C.use(TEN, f"pa{lc}")
                            ins = TEN.matmul(
                                pa[lc][:, :],
                                kv[:, 8192 + kt * 512 + lc * 128: 8192 + kt * 512 + lc * 128 + 128],
                                ybuf[:, 2560 + sb * 512: 2560 + (sb + 1) * 512],
                                start=(kt == 0), stop=(kt == nkt - 1))
                            if kt == nkt - 1:
                                p_ctx_last[lc] = C.inc(ins, "pe")
                        pm = C.cnt["pe"] if kt == nkt - 1 else C.inc(ins, "pe")
                        C.rel(f"E{sb}", "pe", pm)
                    C.use(VEC, "E0")
                    v_es = C.inc(VEC.tensor_copy(ybuf[:, 2560:3072], xf[:, 2048:2560]), "ve")
                    C.rel("E0", "ve", v_es)
                    if h + 1 < H:
                        p_qh[h + 1] = emit_qh(h + 1)
                    C.use(TEN, "pz")
                    C.wait(TEN, "ve", v_es)
                    p_z = C.inc(TEN.matmul(pz[0:1, :], ones[:, 0:1], ybuf[:, 2560:3072],
                                           start=True, stop=True), "pe")
                    C.wait(VEC, "pe", p_z)
                    v = C.inc(VEC.reciprocal(rzf[0:1, :], pz[0:1, :]), "ve")
                    C.wait(VEC, "ve", v)
                    v_rz = C.inc(VEC.tensor_copy(rzb[0:1, :], rzf[0:1, :]), "ve")
                    C.wait(TEN, "ve", v_rz)
                    p_bc = C.inc(TEN.matmul(pz[:, :], ones[0:1, 0:128], rzb[0:1, :],
                                            start=True, stop=True), "pe")
                    C.wait(ACT, "pe", p_bc)
                    C.use(ACT, "bcast")
                    a_bc = C.inc(ACT.copy(ybuf[:, 3584:4096], pz[:, :]), "ac")
                    C.rel("pz", "ac", a_bc)
                    v_ctx = 0
                    for lc in range(4):
                        C.wait(VEC, "ac", a_bc)
                        C.wait(VEC, "pe", p_ctx_last[lc])
                        C.use(VEC, "ctx")
                        v_ctx = C.inc(VEC.tensor_tensor(
                            ybuf[:, 4096 + lc * 512: 4096 + (lc + 1) * 512],
                            pa[lc][:, :], ybuf[:, 3584:4096], ALU.mult), "ve")
                        C.rel(f"pa{lc}", "ve", v_ctx)
                    C.use(TEN, "ps1")
                    C.wait(TEN, "ve", v_ctx)
                    for lc in range(4):
                        ins = TEN.matmul(
                            ps[1][:, :], vup_sl[:, lc * 128:(lc + 1) * 128],
                            ybuf[:, 4096 + lc * 512: 4096 + (lc + 1) * 512],
                            start=(lc == 0), stop=(lc == 3))
                    p_o = C.inc(ins, "pe")
                    C.rel(f"q2l{sl}", "pe", p_o)
                    C.rel(f"vup{sl}", "pe", p_o)
                    C.rel("ctx", "pe", p_o)
                    C.rel("bcast", "pe", p_o)
                    C.rel("qlat", "pe", p_o)
                    C.wait(ACT, "pe", p_o)
                    C.use(ACT, "OT")
                    a_o = C.inc(ACT.copy(OT[:, h * 512:(h + 1) * 512], ps[1][:, :]), "ac")
                    C.rel("ps1", "ac", a_o)
                a_o_all = a_o

                def load_wout(ec):
                    sl = ec % 2
                    C.use(SYN, f"wq{sl}")
                    return ddma(SYN, "qd", wbuf[:, sl * 2048:(sl + 1) * 2048],
                                wb["wout"][ec, :, :])

                woarr = {0: load_wout(0)}
                ao_done = 0
                for ec in range(16):
                    sl = ec % 2
                    if ec + 1 < 16:
                        woarr[ec + 1] = load_wout(ec + 1)
                    bank = f"ps{sl}"
                    C.use(TEN, bank)
                    C.wait(TEN, "qd", woarr[ec])
                    C.wait(TEN, "ac", a_o_all)
                    for hc in range(16):
                        ins = TEN.matmul(
                            ps[sl][:, :],
                            wbuf[:, sl * 2048 + hc * 128: sl * 2048 + hc * 128 + 128],
                            OT[:, hc * 512:(hc + 1) * 512],
                            start=(hc == 0), stop=(hc == 15))
                    p_ao = C.inc(ins, "pe")
                    C.rel(f"wq{sl}", "pe", p_ao)
                    C.wait(ACT, "pe", p_ao)
                    C.use(ACT, "aoT")
                    a_ao = C.inc(ACT.copy(ybuf[:, 0:512], ps[sl][:, :]), "ac")
                    C.rel(bank, "ac", a_ao)
                    C.use(TEN, "pt")
                    C.wait(TEN, "ac", a_ao)
                    for j in range(4):
                        ins = TEN.transpose(pt[:, j * 128:(j + 1) * 128],
                                            ybuf[:, j * 128:(j + 1) * 128], ident[:, :])
                    p_t = C.inc(ins, "pe")
                    C.rel("aoT", "pe", p_t)
                    C.wait(VEC, "pe", p_t)
                    for j in range(4):
                        ao_done = C.inc(VEC.tensor_copy(
                            aux[:, j * 2048 + ec * 128: j * 2048 + ec * 128 + 128],
                            pt[:, j * 128:(j + 1) * 128]), "ve")
                    C.rel("pt", "ve", ao_done)
                C.rel("OT", "pe", p_ao)

                for j in range(4):
                    trow = qc * 512 + j * 128
                    C.use(SYN, "xf0")
                    ld = ddma(SYN, "qd", xf[:, 0:2048], x_d[b, trow:trow + 128, :])
                    ao_tile = aux[:, j * 2048:(j + 1) * 2048]
                    v = ln_tile(ao_tile, xf[:, 4096:6144], xf[:, 4096:6144],
                                pre_wait=("ve", ao_done))
                    C.wait(VEC, "qd", ld)
                    C.use(VEC, "ytile")
                    vy = C.inc(VEC.tensor_add(ybuf[:, 2048:4096],
                                              xf[:, 4096:6144], xf[:, 0:2048]), "ve")
                    C.rel("xf0", "ve", vy)
                    C.wait(ACT, "ve", vy)
                    st = ddma(ACT, "qs", yscr[b, trow:trow + 128, :], ybuf[:, 2048:4096])
                    C.rel("ytile", "qs", st)
        C.end_block()

        with nc.Block():
            qs_guard()
            for s in range(4):
                tbase = s * 512
                C.wait(SYN, "qs", C.cnt["qs"])
                ln2_done = 0
                for j in range(4):
                    C.use(SYN, "ybufR")
                    ld = ddma(SYN, "qd", ybuf[:, j * 2048:(j + 1) * 2048],
                              yscr[b, tbase + j * 128: tbase + j * 128 + 128, :])
                    C.wait(VEC, "qd", ld)
                    v = ln_tile(ybuf[:, j * 2048:(j + 1) * 2048],
                                xf[:, 4096:6144], aux[:, 0:2048], out_war="auxtile")
                    C.use(VEC, "OT")
                    lv, pe_src = transpose_tile(
                        lambda ec: aux[:, ec * 128:(ec + 1) * 128],
                        OT, lambda ec: ec * 512 + j * 128, ("ve", v))
                    C.rel("auxtile", "pe", pe_src)
                    ln2_done = lv

                def load_gu(ic):
                    sl = ic % 2
                    C.use(SYN, "wd0")
                    C.use(SYN, f"wg{sl}")
                    ddma(SYN, "qd", wbuf[:, sl * 2048:(sl + 1) * 2048], wb["wg"][ic, :, :])
                    C.use(SYN, f"wu{sl}")
                    return ddma(SYN, "qd", wbuf[:, 4096 + sl * 2048: 4096 + (sl + 1) * 2048],
                                wb["wu"][ic, :, :])

                guarr = {0: load_gu(0)}
                h_done = 0
                for ic in range(64):
                    sl = ic % 2
                    if ic + 1 < 64:
                        guarr[ic + 1] = load_gu(ic + 1)
                    gb, ub = 2 * sl, 2 * sl + 1
                    C.wait(TEN, "qd", guarr[ic])
                    C.wait(TEN, "ve", ln2_done)
                    C.use(TEN, f"pa{gb}")
                    for ec in range(16):
                        ins = TEN.matmul(
                            pa[gb][:, :],
                            wbuf[:, sl * 2048 + ec * 128: sl * 2048 + ec * 128 + 128],
                            OT[:, ec * 512:(ec + 1) * 512],
                            start=(ec == 0), stop=(ec == 15))
                    p_g = C.inc(ins, "pe")
                    C.use(TEN, f"pa{ub}")
                    for ec in range(16):
                        ins = TEN.matmul(
                            pa[ub][:, :],
                            wbuf[:, 4096 + sl * 2048 + ec * 128: 4096 + sl * 2048 + ec * 128 + 128],
                            OT[:, ec * 512:(ec + 1) * 512],
                            start=(ec == 0), stop=(ec == 15))
                    p_u = C.inc(ins, "pe")
                    C.rel(f"wg{sl}", "pe", p_g)
                    C.rel(f"wu{sl}", "pe", p_u)
                    C.wait(ACT, "pe", p_g)
                    C.use(ACT, "silu")
                    a_si = C.inc(ACT.activation(aux[:, 2048:2560], pa[gb][:, :], AF.Silu), "ac")
                    C.rel(f"pa{gb}", "ac", a_si)
                    C.wait(VEC, "ac", a_si)
                    C.wait(VEC, "pe", p_u)
                    C.use(VEC, "hT")
                    h_done = C.inc(VEC.tensor_tensor(
                        ln1T[:, ic * 512:(ic + 1) * 512],
                        aux[:, 2048:2560], pa[ub][:, :], ALU.mult), "ve")
                    C.rel(f"pa{ub}", "ve", h_done)
                    C.rel("silu", "ve", h_done)
                C.rel("OT", "pe", p_u)

                sp_done_c = []
                for c in range(16):
                    tiles = [ln1T[:, (c + 16 * pl) * 512:(c + 16 * pl + 1) * 512]
                             for pl in range(4)]
                    C.wait(ACT, "ve", h_done if c == 0 else C.cnt["ve"])
                    a_abs = 0
                    for pl in range(4):
                        a_abs = C.inc(ACT.activation(
                            aux[:, pl * 512:(pl + 1) * 512], tiles[pl], AF.Abs), "ac")
                    C.wait(VEC, "ac", a_abs)
                    prs = {}
                    pairs = [(0, 1), (0, 2), (0, 3), (1, 2), (1, 3), (2, 3)]
                    vv = 0
                    for k, (pA, pB) in enumerate(pairs):
                        vv = C.inc(VEC.tensor_tensor(
                            aux[:, 2048 + k * 512: 2048 + (k + 1) * 512],
                            aux[:, pA * 512:(pA + 1) * 512],
                            aux[:, pB * 512:(pB + 1) * 512], ALU.is_ge), "ve")
                        prs[(pA, pB)] = aux[:, 2048 + k * 512: 2048 + (k + 1) * 512]
                    X1 = aux[:, 5120:5632]
                    X2 = aux[:, 5632:6144]
                    C.wait(VEC, "ve", vv)
                    v = C.inc(VEC.tensor_add(X1, prs[(0, 1)], prs[(0, 2)]), "ve")
                    C.wait(VEC, "ve", v)
                    v = C.inc(VEC.scalar_tensor_tensor(X2, prs[(0, 3)], 0.0, X1,
                                                       op0=ALU.add, op1=ALU.add), "ve")
                    C.wait(VEC, "ve", v)
                    C.inc(VEC.scalar_tensor_tensor(tiles[0], X2, 2.0, tiles[0],
                                                   op0=ALU.is_ge, op1=ALU.mult), "ve")
                    v = C.inc(VEC.tensor_add(X1, prs[(1, 2)], prs[(1, 3)]), "ve")
                    C.wait(VEC, "ve", v)
                    v = C.inc(VEC.scalar_tensor_tensor(X2, X1, 0.0, prs[(0, 1)],
                                                       op0=ALU.add, op1=ALU.subtract), "ve")
                    C.wait(VEC, "ve", v)
                    C.inc(VEC.scalar_tensor_tensor(tiles[1], X2, 1.0, tiles[1],
                                                   op0=ALU.is_ge, op1=ALU.mult), "ve")
                    v = C.inc(VEC.tensor_add(X1, prs[(0, 2)], prs[(1, 2)]), "ve")
                    C.wait(VEC, "ve", v)
                    v = C.inc(VEC.scalar_tensor_tensor(X2, prs[(2, 3)], 0.0, X1,
                                                       op0=ALU.add, op1=ALU.subtract), "ve")
                    C.wait(VEC, "ve", v)
                    C.inc(VEC.scalar_tensor_tensor(tiles[2], X2, 0.0, tiles[2],
                                                   op0=ALU.is_ge, op1=ALU.mult), "ve")
                    v = C.inc(VEC.tensor_add(X1, prs[(0, 3)], prs[(1, 3)]), "ve")
                    C.wait(VEC, "ve", v)
                    v = C.inc(VEC.scalar_tensor_tensor(X2, X1, 0.0, prs[(2, 3)],
                                                       op0=ALU.add, op1=ALU.add), "ve")
                    C.wait(VEC, "ve", v)
                    C.inc(VEC.scalar_tensor_tensor(tiles[3], X2, 1.0, tiles[3],
                                                   op0=ALU.is_le, op1=ALU.mult), "ve")
                    sp_done_c.append(C.cnt["ve"])
                C.rel("auxtile", "ac", a_abs)

                def load_wd(ec):
                    sl = ec % 2
                    C.use(SYN, f"wd{sl}")
                    if sl == 0:
                        for r in ("wg0", "wg1", "wu0", "wu1"):
                            C.use(SYN, r)
                    return ddma(SYN, "qd", wbuf[:, sl * 8192:(sl + 1) * 8192],
                                wb["wd"][ec, :, :])

                wdarr = {0: load_wd(0)}
                f_done = 0
                for ec in range(16):
                    sl = ec % 2
                    if ec + 1 < 16:
                        wdarr[ec + 1] = load_wd(ec + 1)
                    bank = f"ps{sl}"
                    C.use(TEN, bank)
                    C.wait(TEN, "qd", wdarr[ec])
                    first = True
                    for c in range(16):
                        C.wait(TEN, "ve", sp_done_c[c])
                        for pl in range(4):
                            ic = c + 16 * pl
                            ins = TEN.matmul(
                                ps[sl][:, :],
                                wbuf[:, sl * 8192 + ic * 128: sl * 8192 + ic * 128 + 128],
                                ln1T[:, ic * 512:(ic + 1) * 512],
                                start=first, stop=(c == 15 and pl == 3))
                            first = False
                    p_f = C.inc(ins, "pe")
                    C.rel(f"wd{sl}", "pe", p_f)
                    C.wait(ACT, "pe", p_f)
                    C.use(ACT, "fT")
                    a_f = C.inc(ACT.copy(kv[:, 8192:8704], ps[sl][:, :]), "ac")
                    C.rel(bank, "ac", a_f)
                    C.use(TEN, "pt")
                    C.wait(TEN, "ac", a_f)
                    for j in range(4):
                        ins = TEN.transpose(pt[:, j * 128:(j + 1) * 128],
                                            kv[:, 8192 + j * 128: 8192 + (j + 1) * 128],
                                            ident[:, :])
                    p_t = C.inc(ins, "pe")
                    C.rel("fT", "pe", p_t)
                    C.wait(VEC, "pe", p_t)
                    for j in range(4):
                        f_done = C.inc(VEC.tensor_copy(
                            kv[:, j * 2048 + ec * 128: j * 2048 + ec * 128 + 128],
                            pt[:, j * 128:(j + 1) * 128]), "ve")
                    C.rel("pt", "ve", f_done)
                C.rel("hT", "pe", p_f)

                for j in range(4):
                    trow = tbase + j * 128
                    f_tile = kv[:, j * 2048:(j + 1) * 2048]
                    v = ln_tile(f_tile, xf[:, 4096:6144], xf[:, 4096:6144],
                                pre_wait=("ve", f_done))
                    vc = C.inc(VEC.tensor_copy(xf[:, 2048:4096],
                                               ybuf[:, j * 2048:(j + 1) * 2048]), "ve")
                    C.wait(VEC, "ve", vc)
                    C.use(VEC, "outstage")
                    vo = C.inc(VEC.tensor_add(xf[:, 0:2048], xf[:, 4096:6144],
                                              xf[:, 2048:4096]), "ve")
                    C.wait(ACT, "ve", vo)
                    st = ddma(ACT, "qs", out_d[b, trow:trow + 128, :], xf[:, 0:2048])
                    C.rel("outstage", "qs", st)
                C.rel("ybufR", "ve", vo)
        C.end_block()

    with nc.Block():
        SYN.wait_ge(C.sems["qs"], C.cnt["qs"])
    C.end_block()
    return nc


_NC_CACHE = {}


def kernel(L0_attn_hidden_states, L0_attn_Wq, L0_attn_Wk_down, L0_attn_Wv_down,
           L0_attn_q_to_latent, L0_attn_v_up, L0_attn_Wout,
           L0_ffn_w_gate, L0_ffn_w_up, L0_ffn_w_down):
    x = np.ascontiguousarray(np.asarray(L0_attn_hidden_states, dtype=np.float32))
    Wq = np.asarray(L0_attn_Wq, dtype=np.float32)
    Wk = np.asarray(L0_attn_Wk_down, dtype=np.float32)
    Wv = np.asarray(L0_attn_Wv_down, dtype=np.float32)
    q2l = np.asarray(L0_attn_q_to_latent, dtype=np.float32)
    vup = np.asarray(L0_attn_v_up, dtype=np.float32)
    Wout = np.asarray(L0_attn_Wout, dtype=np.float32)
    wg = np.asarray(L0_ffn_w_gate, dtype=np.float32)
    wu = np.asarray(L0_ffn_w_up, dtype=np.float32)
    wd = np.asarray(L0_ffn_w_down, dtype=np.float32)

    pidx = 4 * (np.arange(I) % 2048) + np.arange(I) // 2048
    wg_p = wg[:, pidx]
    wu_p = wu[:, pidx]
    wd_p = wd[pidx, :]

    wq_host = np.ascontiguousarray(
        Wq.reshape(16, 128, 16, 128).transpose(2, 1, 0, 3).reshape(16, 128, 2048))
    wk_host = np.ascontiguousarray(
        Wk.reshape(16, 128, 512).transpose(1, 0, 2).reshape(128, 8192))
    wv_host = np.ascontiguousarray(
        Wv.reshape(16, 128, 512).transpose(1, 0, 2).reshape(128, 8192))
    q2l_host = np.ascontiguousarray(q2l.reshape(16, 128, 512))
    vup_host = np.ascontiguousarray(
        vup.reshape(16, 4, 128, 128).transpose(0, 2, 1, 3).reshape(16, 128, 512))
    wout_host = np.ascontiguousarray(
        Wout.reshape(16, 128, 16, 128).transpose(2, 1, 0, 3).reshape(16, 128, 2048))
    wg_host = np.ascontiguousarray(
        wg_p.reshape(16, 128, 64, 128).transpose(2, 1, 0, 3).reshape(64, 128, 2048))
    wu_host = np.ascontiguousarray(
        wu_p.reshape(16, 128, 64, 128).transpose(2, 1, 0, 3).reshape(64, 128, 2048))
    wd_host = np.ascontiguousarray(
        wd_p.reshape(64, 128, 16, 128).transpose(2, 1, 0, 3).reshape(16, 128, 8192))

    ii = np.arange(128)[:, None]
    jj = np.arange(512)[None, :]
    masks = np.concatenate(
        [np.where(jj - 128 * m >= ii, 1.0, 0.0) for m in range(4)],
        axis=1).astype(ml_dtypes.bfloat16)
    ident = np.eye(128).astype(ml_dtypes.bfloat16)
    ones = np.ones((128, 128), dtype=ml_dtypes.bfloat16)

    if "nc" not in _NC_CACHE:
        _NC_CACHE["nc"] = build_nc()
    nc = _NC_CACHE["nc"]

    base = {
        "wq": wq_host, "wk": wk_host, "wv": wv_host, "q2l": q2l_host,
        "vup": vup_host, "wout": wout_host, "wg": wg_host, "wu": wu_host,
        "wd": wd_host, "masks": masks, "ident": ident, "ones": ones,
        "cvals": np.tile(np.array([[EPS, -0.5, SCALE, 0.0, 0.0, 0.0, 0.0, 0.0]],
                          dtype=np.float32), (128, 1)),
    }
    in_maps = []
    for c in range(8):
        m = dict(base)
        m["x"] = np.ascontiguousarray(x[2 * c: 2 * c + 2])
        in_maps.append(m)

    res = run_bass_kernel_spmd(nc, in_maps, core_ids=list(range(8)))
    out = np.concatenate([np.asarray(r["out"]) for r in res.results], axis=0)
    return out.astype(np.float32)


# version 08 (final)
# speedup vs baseline: 1.0346x; 1.0346x over previous
import numpy as np
import ml_dtypes
import concourse.bass as bass
import concourse.mybir as mybir
from concourse.bass_utils import run_bass_kernel_spmd

F32 = mybir.dt.float32
BF16 = mybir.dt.bfloat16
AF = mybir.ActivationFunctionType
ALU = mybir.AluOpType
AX = mybir.AxisListType

E = 2048
T = 2048
H = 16
LAT = 512
I = 8192
NB = 2
SCALE = LAT ** -0.5
EPS = 1e-5
NEG = -1e9

COMPUTE_SEMS = ("pe", "ac", "ve")


class Ctl:

    def __init__(self, nc, engines):
        self.nc = nc
        self.engines = engines
        self.sems = {}
        self.cnt = {}
        self.waited = {}
        self.free = {}

    def sem(self, name):
        self.sems[name] = self.nc.alloc_semaphore(name)
        self.cnt[name] = 0

    def inc(self, instr, s, n=1):
        instr.then_inc(self.sems[s], n)
        self.cnt[s] += n
        return self.cnt[s]

    def wait(self, eng, s, v):
        if v <= 0:
            return
        assert v <= self.cnt[s], (s, v, self.cnt[s])
        k = (id(eng), s)
        if self.waited.get(k, 0) >= v:
            return
        eng.wait_ge(self.sems[s], v)
        self.waited[k] = v

    def use(self, eng, res):
        cond = self.free.get(res)
        if cond is not None:
            self.wait(eng, cond[0], cond[1])

    def rel(self, res, s, v):
        self.free[res] = (s, v)

    def end_block(self):
        for eng in self.engines:
            for s in COMPUTE_SEMS:
                self.waited[(id(eng), s)] = self.cnt[s]
        self.free = {}


def build_nc():
    nc = bass.Bass("TRN2", target_bir_lowering=False)
    TEN, VEC, ACT, SYN = nc.tensor, nc.vector, nc.scalar, nc.sync

    x_d = nc.dram_tensor("x", [NB, T, E], F32, kind="ExternalInput")
    out_d = nc.dram_tensor("out", [NB, T, E], F32, kind="ExternalOutput")
    wf, wb = {}, {}
    wshapes = {
        "wq": [16, 128, 2048], "wk": [128, 8192], "wv": [128, 8192],
        "q2l": [16, 128, 512], "vup": [16, 128, 512], "wout": [16, 128, 2048],
        "wg": [64, 128, 2048], "wu": [64, 128, 2048], "wd": [16, 128, 8192],
    }
    for n, sh in wshapes.items():
        wf[n] = nc.dram_tensor(n, sh, F32, kind="ExternalInput")
        wb[n] = nc.dram_tensor(n + "b", sh, BF16, kind="Internal")
    masks_d = nc.dram_tensor("masks", [128, 2048], BF16, kind="ExternalInput")
    ident_d = nc.dram_tensor("ident", [128, 128], BF16, kind="ExternalInput")
    ones_d = nc.dram_tensor("ones", [128, 128], BF16, kind="ExternalInput")
    cvals_d = nc.dram_tensor("cvals", [128, 8], F32, kind="ExternalInput")
    yscr = nc.dram_tensor("yscr", [NB, T, E], BF16, kind="Internal")

    ln1T = nc.alloc_sbuf_tensor("ln1T", [128, 32768], BF16)
    OT = nc.alloc_sbuf_tensor("OT", [128, 8192], BF16)
    kv = nc.alloc_sbuf_tensor("kv", [128, 16384], BF16)
    wbuf = nc.alloc_sbuf_tensor("wbuf", [128, 16384], BF16)
    ybuf = nc.alloc_sbuf_tensor("ybuf", [128, 8192], BF16)
    aux = nc.alloc_sbuf_tensor("aux", [128, 8192], BF16)
    xf = nc.alloc_sbuf_tensor("xf", [128, 6144], F32)
    consts = nc.alloc_sbuf_tensor("cmask", [128, 2048], BF16)
    ident = nc.alloc_sbuf_tensor("ident_sb", [128, 128], BF16)
    ones = nc.alloc_sbuf_tensor("ones_sb", [128, 128], BF16)
    stats = nc.alloc_sbuf_tensor("stats", [128, 16], F32)
    rzf = nc.alloc_sbuf_tensor("rzf", [1, 512], F32)
    rzb = nc.alloc_sbuf_tensor("rzb", [1, 512], BF16)
    cvals = nc.alloc_sbuf_tensor("cvals_sb", [128, 8], F32)
    nc.const_aps.register(F32, EPS, cvals[:, 0:1])
    nc.const_aps.register(F32, -0.5, cvals[:, 1:2])
    nc.const_aps.register(F32, SCALE, cvals[:, 2:3])

    pa = [nc.alloc_psum_tensor(f"pa{i}", [128, 512], F32) for i in range(4)]
    ps = [nc.alloc_psum_tensor(f"ps{i}", [128, 512], F32) for i in range(2)]
    pz = nc.alloc_psum_tensor("pz", [128, 512], F32)
    pt = nc.alloc_psum_tensor("pt", [128, 1024], BF16)

    engines = [TEN, VEC, ACT, SYN]
    C = Ctl(nc, engines)
    for s in ("qd", "qs", "pe", "ac", "ve"):
        C.sem(s)

    def ddma(eng, qsem, out, in_):
        return C.inc(eng.dma_start(out=out, in_=in_), qsem, 16)

    def qs_guard():
        for eng in (SYN, ACT, VEC):
            C.wait(eng, "qs", C.cnt["qs"])

    def ln_tile(in_tile, sqdump, out_ap, pre_wait=None, out_war=None):
        if pre_wait is not None:
            C.wait(VEC, pre_wait[0], pre_wait[1])
        C.inc(VEC.scalar_tensor_tensor(
            out=sqdump, in0=in_tile, scalar=1.0, in1=in_tile,
            op0=ALU.mult, op1=ALU.mult, accum_out=stats[:, 1:2]), "ve")
        v = C.inc(VEC.reduce_sum(stats[:, 0:1], in_tile, axis=AX.X), "ve")
        C.wait(VEC, "ve", v)
        v = C.inc(VEC.tensor_scalar_mul(stats[:, 2:3], stats[:, 0:1], 1.0 / E), "ve")
        C.wait(VEC, "ve", v)
        v = C.inc(VEC.tensor_mul(stats[:, 3:4], stats[:, 2:3], stats[:, 2:3]), "ve")
        C.wait(VEC, "ve", v)
        v = C.inc(VEC.scalar_tensor_tensor(
            out=stats[:, 4:5], in0=stats[:, 1:2], scalar=1.0 / E, in1=stats[:, 3:4],
            op0=ALU.mult, op1=ALU.subtract), "ve")
        C.wait(ACT, "ve", v)
        a = C.inc(ACT.activation(stats[:, 5:6], stats[:, 4:5], AF.Ln, bias=EPS), "ac")
        C.wait(ACT, "ac", a)
        a = C.inc(ACT.activation(stats[:, 6:7], stats[:, 5:6], AF.Exp, scale=-0.5), "ac")
        C.wait(VEC, "ac", a)
        if out_war is not None:
            C.use(VEC, out_war)
        v = C.inc(VEC.tensor_scalar(
            out=out_ap, in0=in_tile, scalar1=stats[:, 2:3], scalar2=stats[:, 6:7],
            op0=ALU.subtract, op1=ALU.mult), "ve")
        return v

    def transpose_tile(src_col_fn, dest_sb, dest_col_fn, src_gate):
        last_v = None
        for g in range(4):
            C.use(TEN, "pt")
            C.wait(TEN, src_gate[0], src_gate[1])
            for j in range(4):
                ins = TEN.transpose(pt[:, j * 128:(j + 1) * 128],
                                    src_col_fn(g * 4 + j), ident[:, :])
            p = C.inc(ins, "pe")
            C.wait(VEC, "pe", p)
            for j in range(4):
                dc = dest_col_fn(g * 4 + j)
                last_v = C.inc(VEC.tensor_copy(
                    dest_sb[:, dc:dc + 128], pt[:, j * 128:(j + 1) * 128]), "ve")
            C.rel("pt", "ve", last_v)
        return last_v, C.cnt["pe"]

    with nc.Block():
        ddma(SYN, "qd", consts[:, :], masks_d[:, :])
        ddma(SYN, "qd", ident[:, :], ident_d[:, :])
        ddma(SYN, "qd", ones[:, :], ones_d[:, :])
        ddma(SYN, "qd", cvals[:, :], cvals_d[:, :])

        xin = xf[:, 0:4096]
        bout = [aux[:, 0:4096], aux[:, 4096:8192]]
        ci = 0
        for n, sh in wshapes.items():
            nel = int(np.prod(sh))
            nch = nel // (128 * 4096)
            if len(sh) == 3:
                src = wf[n][:, :, :].rearrange("a b c -> (a b c)")
                dst = wb[n][:, :, :].rearrange("a b c -> (a b c)")
            else:
                src = wf[n][:, :].rearrange("a b -> (a b)")
                dst = wb[n][:, :].rearrange("a b -> (a b)")
            src = src.rearrange("(n p w) -> n p w", p=128, w=4096)
            dst = dst.rearrange("(n p w) -> n p w", p=128, w=4096)
            for i in range(nch):
                sl = ci % 2
                C.use(SYN, "p0in")
                ld = ddma(SYN, "qd", xin, src[i, :, :])
                eng, esem = (ACT, "ac") if sl == 0 else (VEC, "ve")
                C.wait(eng, "qd", ld)
                C.use(eng, f"p0out{sl}")
                if sl == 0:
                    cv = C.inc(ACT.copy(bout[sl], xin), "ac")
                else:
                    cv = C.inc(VEC.tensor_copy(bout[sl], xin), "ve")
                C.rel("p0in", esem, cv)
                C.wait(ACT, esem, cv)
                st = ddma(ACT, "qs", dst[i, :, :], bout[sl])
                C.rel(f"p0out{sl}", "qs", st)
                ci += 1
        SYN.wait_ge(C.sems["qs"], C.cnt["qs"])
        SYN.wait_ge(C.sems["qd"], C.cnt["qd"])
        C.waited[(id(SYN), "qs")] = C.cnt["qs"]
        C.waited[(id(SYN), "qd")] = C.cnt["qd"]
    C.end_block()

    for b in range(NB):
        with nc.Block():
            qs_guard()
            lwk = ddma(ACT, "qs", wbuf[:, 0:8192], wb["wk"][:, :])
            lwv = ddma(ACT, "qs", wbuf[:, 8192:16384], wb["wv"][:, :])
            lv = {}
            rot = 0
            for tt in range(16):
                C.use(SYN, "xf0")
                ld = ddma(SYN, "qd", xf[:, 0:2048], x_d[b, tt * 128:(tt + 1) * 128, :])
                C.wait(VEC, "qd", ld)
                v = ln_tile(xf[:, 0:2048], xf[:, 4096:6144], aux[:, 0:2048],
                            out_war="auxtile")
                C.rel("xf0", "ve", v)
                lv[tt], pe_src = transpose_tile(
                    lambda ec: aux[:, ec * 128:(ec + 1) * 128],
                    ln1T, lambda ec: ec * 2048 + tt * 128, ("ve", v))
                C.rel("auxtile", "pe", pe_src)
                bank = f"pa{rot % 4}"
                C.use(TEN, bank)
                C.wait(TEN, "qs", lwv)
                C.wait(TEN, "ve", lv[tt])
                for ec in range(16):
                    ins = TEN.matmul(
                        pa[rot % 4][:, :],
                        ln1T[:, ec * 2048 + tt * 128: ec * 2048 + tt * 128 + 128],
                        wbuf[:, 8192 + ec * 512: 8192 + (ec + 1) * 512],
                        start=(ec == 0), stop=(ec == 15))
                p = C.inc(ins, "pe")
                C.wait(ACT, "pe", p)
                a = C.inc(ACT.copy(kv[:, 8192 + tt * 512: 8192 + (tt + 1) * 512],
                                   pa[rot % 4][:, :]), "ac")
                C.rel(bank, "ac", a)
                rot += 1
                if tt % 4 == 3:
                    tc = tt // 4
                    for lc in range(4):
                        bank = f"pa{rot % 4}"
                        C.use(TEN, bank)
                        for ec in range(16):
                            ins = TEN.matmul(
                                pa[rot % 4][:, :],
                                wbuf[:, ec * 512 + lc * 128: ec * 512 + lc * 128 + 128],
                                ln1T[:, ec * 2048 + tc * 512: ec * 2048 + (tc + 1) * 512],
                                start=(ec == 0), stop=(ec == 15))
                        p = C.inc(ins, "pe")
                        C.wait(ACT, "pe", p)
                        a = C.inc(ACT.copy(
                            kv[:, lc * 2048 + tc * 512: lc * 2048 + (tc + 1) * 512],
                            pa[rot % 4][:, :]), "ac")
                        C.rel(bank, "ac", a)
                        rot += 1
        C.end_block()

        with nc.Block():
            qs_guard()
            for qc in range(4):
                nkt = 4 * qc + 4
                C.wait(ACT, "qs", C.cnt["qs"])
                C.wait(VEC, "qs", C.cnt["qs"])

                def load_head(h):
                    sl = h % 2
                    C.use(SYN, f"wq{sl}")
                    ddma(SYN, "qd", wbuf[:, sl * 2048:(sl + 1) * 2048], wb["wq"][h, :, :])
                    C.use(SYN, f"q2l{sl}")
                    ddma(SYN, "qd", wbuf[:, 4096 + sl * 512: 4096 + (sl + 1) * 512],
                         wb["q2l"][h, :, :])
                    C.use(SYN, f"vup{sl}")
                    return ddma(SYN, "qd", wbuf[:, 5120 + sl * 512: 5120 + (sl + 1) * 512],
                                wb["vup"][h, :, :])

                warr = {0: load_head(0)}

                def emit_qh(h):
                    sl = h % 2
                    C.wait(TEN, "qd", warr[h])
                    C.use(TEN, "ps0")
                    for ec in range(16):
                        ins = TEN.matmul(
                            ps[0][:, :],
                            wbuf[:, sl * 2048 + ec * 128: sl * 2048 + ec * 128 + 128],
                            ln1T[:, ec * 2048 + qc * 512: ec * 2048 + (qc + 1) * 512],
                            start=(ec == 0), stop=(ec == 15))
                    p = C.inc(ins, "pe")
                    C.rel(f"wq{sl}", "pe", p)
                    return p

                p_qh = {0: emit_qh(0)}
                a_o = 0
                for h in range(H):
                    sl = h % 2
                    if h + 1 < H:
                        warr[h + 1] = load_head(h + 1)
                    q2l_sl = wbuf[:, 4096 + sl * 512: 4096 + (sl + 1) * 512]
                    vup_sl = wbuf[:, 5120 + sl * 512: 5120 + (sl + 1) * 512]
                    C.wait(ACT, "pe", p_qh[h])
                    C.use(ACT, "qh")
                    a_qh = C.inc(ACT.copy(ybuf[:, 0:512], ps[0][:, :]), "ac")
                    C.rel("ps0", "ac", a_qh)
                    a_ql = 0
                    for lc in range(4):
                        C.use(TEN, f"pa{lc}")
                        C.wait(TEN, "ac", a_qh)
                        pm = C.inc(TEN.matmul(
                            pa[lc][:, :], q2l_sl[:, lc * 128:(lc + 1) * 128],
                            ybuf[:, 0:512], start=True, stop=True), "pe")
                        if lc == 3:
                            C.rel("qh", "pe", pm)
                        C.wait(ACT, "pe", pm)
                        C.use(ACT, "qlat")
                        a_ql = C.inc(ACT.activation(
                            ybuf[:, 512 + lc * 512: 512 + (lc + 1) * 512],
                            pa[lc][:, :], AF.Copy, scale=SCALE), "ac")
                        C.rel(f"pa{lc}", "ac", a_ql)
                    p_ctx_last = {}
                    p_z = 0

                    def emit_S(kt):
                        sb = kt % 2
                        C.use(TEN, f"ps{sb}")
                        C.wait(TEN, "ac", a_ql)
                        for lc in range(4):
                            ins = TEN.matmul(
                                ps[sb][:, :],
                                kv[:, lc * 2048 + kt * 128: lc * 2048 + kt * 128 + 128],
                                ybuf[:, 512 + lc * 512: 512 + (lc + 1) * 512],
                                start=(lc == 0), stop=(lc == 3))
                        return C.inc(ins, "pe")

                    p_s = emit_S(0)
                    for kt in range(nkt):
                        sb = kt % 2
                        C.wait(ACT, "pe", p_s)
                        C.use(ACT, f"E{sb}")
                        a_e = C.inc(ACT.activation(
                            ybuf[:, 2560 + sb * 512: 2560 + (sb + 1) * 512],
                            ps[sb][:, :], AF.Exp), "ac")
                        C.rel(f"ps{sb}", "ac", a_e)
                        e_gate = ("ac", a_e)
                        C.wait(VEC, "ac", a_e)
                        if kt >= 4 * qc:
                            m = kt - 4 * qc
                            vm = C.inc(VEC.tensor_tensor(
                                ybuf[:, 2560 + sb * 512: 2560 + (sb + 1) * 512],
                                ybuf[:, 2560 + sb * 512: 2560 + (sb + 1) * 512],
                                consts[:, m * 512:(m + 1) * 512], ALU.mult), "ve")
                            e_gate = ("ve", vm)
                        if kt == 0:
                            C.inc(VEC.tensor_copy(
                                xf[:, 2048:2560],
                                ybuf[:, 2560 + sb * 512: 2560 + (sb + 1) * 512]), "ve")
                        else:
                            C.inc(VEC.tensor_add(
                                xf[:, 2048:2560], xf[:, 2048:2560],
                                ybuf[:, 2560 + sb * 512: 2560 + (sb + 1) * 512]), "ve")
                        if kt + 1 < nkt:
                            p_s = emit_S(kt + 1)
                        C.wait(TEN, e_gate[0], e_gate[1])
                        for lc in range(4):
                            if kt == 0:
                                C.use(TEN, f"pa{lc}")
                            ins = TEN.matmul(
                                pa[lc][:, :],
                                kv[:, 8192 + kt * 512 + lc * 128: 8192 + kt * 512 + lc * 128 + 128],
                                ybuf[:, 2560 + sb * 512: 2560 + (sb + 1) * 512],
                                start=(kt == 0), stop=(kt == nkt - 1))
                            if kt == nkt - 1:
                                p_ctx_last[lc] = C.inc(ins, "pe")
                        pm = C.cnt["pe"] if kt == nkt - 1 else C.inc(ins, "pe")
                        C.rel(f"E{sb}", "pe", pm)
                    C.use(VEC, "E0")
                    v_es = C.inc(VEC.tensor_copy(ybuf[:, 2560:3072], xf[:, 2048:2560]), "ve")
                    C.rel("E0", "ve", v_es)
                    if h + 1 < H:
                        p_qh[h + 1] = emit_qh(h + 1)
                    C.use(TEN, "pz")
                    C.wait(TEN, "ve", v_es)
                    p_z = C.inc(TEN.matmul(pz[0:1, :], ones[:, 0:1], ybuf[:, 2560:3072],
                                           start=True, stop=True), "pe")
                    C.wait(VEC, "pe", p_z)
                    v = C.inc(VEC.reciprocal(rzf[0:1, :], pz[0:1, :]), "ve")
                    C.wait(VEC, "ve", v)
                    v_rz = C.inc(VEC.tensor_copy(rzb[0:1, :], rzf[0:1, :]), "ve")
                    C.wait(TEN, "ve", v_rz)
                    p_bc = C.inc(TEN.matmul(pz[:, :], ones[0:1, 0:128], rzb[0:1, :],
                                            start=True, stop=True), "pe")
                    C.wait(ACT, "pe", p_bc)
                    C.use(ACT, "bcast")
                    a_bc = C.inc(ACT.copy(ybuf[:, 3584:4096], pz[:, :]), "ac")
                    C.rel("pz", "ac", a_bc)
                    v_ctx = 0
                    for lc in range(4):
                        C.wait(VEC, "ac", a_bc)
                        C.wait(VEC, "pe", p_ctx_last[lc])
                        C.use(VEC, "ctx")
                        v_ctx = C.inc(VEC.tensor_tensor(
                            ybuf[:, 4096 + lc * 512: 4096 + (lc + 1) * 512],
                            pa[lc][:, :], ybuf[:, 3584:4096], ALU.mult), "ve")
                        C.rel(f"pa{lc}", "ve", v_ctx)
                    C.use(TEN, "ps1")
                    C.wait(TEN, "ve", v_ctx)
                    for lc in range(4):
                        ins = TEN.matmul(
                            ps[1][:, :], vup_sl[:, lc * 128:(lc + 1) * 128],
                            ybuf[:, 4096 + lc * 512: 4096 + (lc + 1) * 512],
                            start=(lc == 0), stop=(lc == 3))
                    p_o = C.inc(ins, "pe")
                    C.rel(f"q2l{sl}", "pe", p_o)
                    C.rel(f"vup{sl}", "pe", p_o)
                    C.rel("ctx", "pe", p_o)
                    C.rel("bcast", "pe", p_o)
                    C.rel("qlat", "pe", p_o)
                    C.wait(ACT, "pe", p_o)
                    C.use(ACT, "OT")
                    a_o = C.inc(ACT.copy(OT[:, h * 512:(h + 1) * 512], ps[1][:, :]), "ac")
                    C.rel("ps1", "ac", a_o)
                a_o_all = a_o

                def load_wout(ec):
                    sl = ec % 2
                    C.use(SYN, f"wq{sl}")
                    return ddma(SYN, "qd", wbuf[:, sl * 2048:(sl + 1) * 2048],
                                wb["wout"][ec, :, :])

                woarr = {0: load_wout(0)}
                ao_done = 0
                for ec in range(16):
                    sl = ec % 2
                    if ec + 1 < 16:
                        woarr[ec + 1] = load_wout(ec + 1)
                    bank = f"ps{sl}"
                    C.use(TEN, bank)
                    C.wait(TEN, "qd", woarr[ec])
                    C.wait(TEN, "ac", a_o_all)
                    for hc in range(16):
                        ins = TEN.matmul(
                            ps[sl][:, :],
                            wbuf[:, sl * 2048 + hc * 128: sl * 2048 + hc * 128 + 128],
                            OT[:, hc * 512:(hc + 1) * 512],
                            start=(hc == 0), stop=(hc == 15))
                    p_ao = C.inc(ins, "pe")
                    C.rel(f"wq{sl}", "pe", p_ao)
                    C.wait(ACT, "pe", p_ao)
                    C.use(ACT, "aoT")
                    a_ao = C.inc(ACT.copy(ybuf[:, 0:512], ps[sl][:, :]), "ac")
                    C.rel(bank, "ac", a_ao)
                    C.use(TEN, "pt")
                    C.wait(TEN, "ac", a_ao)
                    for j in range(4):
                        ins = TEN.transpose(pt[:, j * 128:(j + 1) * 128],
                                            ybuf[:, j * 128:(j + 1) * 128], ident[:, :])
                    p_t = C.inc(ins, "pe")
                    C.rel("aoT", "pe", p_t)
                    C.wait(VEC, "pe", p_t)
                    for j in range(4):
                        ao_done = C.inc(VEC.tensor_copy(
                            aux[:, j * 2048 + ec * 128: j * 2048 + ec * 128 + 128],
                            pt[:, j * 128:(j + 1) * 128]), "ve")
                    C.rel("pt", "ve", ao_done)
                C.rel("OT", "pe", p_ao)

                for j in range(4):
                    trow = qc * 512 + j * 128
                    C.use(SYN, "xf0")
                    ld = ddma(SYN, "qd", xf[:, 0:2048], x_d[b, trow:trow + 128, :])
                    ao_tile = aux[:, j * 2048:(j + 1) * 2048]
                    v = ln_tile(ao_tile, xf[:, 4096:6144], xf[:, 4096:6144],
                                pre_wait=("ve", ao_done))
                    C.wait(VEC, "qd", ld)
                    C.use(VEC, "ytile")
                    vy = C.inc(VEC.tensor_add(ybuf[:, 2048:4096],
                                              xf[:, 4096:6144], xf[:, 0:2048]), "ve")
                    C.rel("xf0", "ve", vy)
                    C.wait(ACT, "ve", vy)
                    st = ddma(ACT, "qs", yscr[b, trow:trow + 128, :], ybuf[:, 2048:4096])
                    C.rel("ytile", "qs", st)
        C.end_block()

        with nc.Block():
            qs_guard()
            for s in range(4):
                tbase = s * 512
                C.wait(SYN, "qs", C.cnt["qs"])
                ln2_done = 0
                for j in range(4):
                    C.use(SYN, "ybufR")
                    ld = ddma(SYN, "qd", ybuf[:, j * 2048:(j + 1) * 2048],
                              yscr[b, tbase + j * 128: tbase + j * 128 + 128, :])
                    C.wait(VEC, "qd", ld)
                    v = ln_tile(ybuf[:, j * 2048:(j + 1) * 2048],
                                xf[:, 4096:6144], aux[:, 0:2048], out_war="auxtile")
                    C.use(VEC, "OT")
                    lv, pe_src = transpose_tile(
                        lambda ec: aux[:, ec * 128:(ec + 1) * 128],
                        OT, lambda ec: ec * 512 + j * 128, ("ve", v))
                    C.rel("auxtile", "pe", pe_src)
                    ln2_done = lv

                def load_gu(ic):
                    sl = ic % 2
                    C.use(SYN, "wd0")
                    C.use(SYN, f"wg{sl}")
                    g = ddma(SYN, "qd", wbuf[:, sl * 2048:(sl + 1) * 2048],
                             wb["wg"][ic, :, :])
                    C.use(ACT, "wd0")
                    C.use(ACT, f"wu{sl}")
                    u = ddma(ACT, "qs", wbuf[:, 4096 + sl * 2048: 4096 + (sl + 1) * 2048],
                             wb["wu"][ic, :, :])
                    return (g, u)

                guarr = {0: load_gu(0)}
                h_done = 0
                for ic in range(64):
                    sl = ic % 2
                    if ic + 1 < 64:
                        guarr[ic + 1] = load_gu(ic + 1)
                    gb, ub = 2 * sl, 2 * sl + 1
                    C.wait(TEN, "qd", guarr[ic][0])
                    C.wait(TEN, "qs", guarr[ic][1])
                    C.wait(TEN, "ve", ln2_done)
                    C.use(TEN, f"pa{gb}")
                    for ec in range(16):
                        ins = TEN.matmul(
                            pa[gb][:, :],
                            wbuf[:, sl * 2048 + ec * 128: sl * 2048 + ec * 128 + 128],
                            OT[:, ec * 512:(ec + 1) * 512],
                            start=(ec == 0), stop=(ec == 15))
                    p_g = C.inc(ins, "pe")
                    C.use(TEN, f"pa{ub}")
                    for ec in range(16):
                        ins = TEN.matmul(
                            pa[ub][:, :],
                            wbuf[:, 4096 + sl * 2048 + ec * 128: 4096 + sl * 2048 + ec * 128 + 128],
                            OT[:, ec * 512:(ec + 1) * 512],
                            start=(ec == 0), stop=(ec == 15))
                    p_u = C.inc(ins, "pe")
                    C.rel(f"wg{sl}", "pe", p_g)
                    C.rel(f"wu{sl}", "pe", p_u)
                    C.wait(ACT, "pe", p_g)
                    C.use(ACT, "silu")
                    a_si = C.inc(ACT.activation(aux[:, 2048:2560], pa[gb][:, :], AF.Silu), "ac")
                    C.rel(f"pa{gb}", "ac", a_si)
                    C.wait(VEC, "ac", a_si)
                    C.wait(VEC, "pe", p_u)
                    C.use(VEC, "hT")
                    h_done = C.inc(VEC.tensor_tensor(
                        ln1T[:, ic * 512:(ic + 1) * 512],
                        aux[:, 2048:2560], pa[ub][:, :], ALU.mult), "ve")
                    C.rel(f"pa{ub}", "ve", h_done)
                    C.rel("silu", "ve", h_done)
                C.rel("OT", "pe", p_u)

                sp_done_c = []
                for c in range(16):
                    tiles = [ln1T[:, (c + 16 * pl) * 512:(c + 16 * pl + 1) * 512]
                             for pl in range(4)]
                    C.wait(ACT, "ve", h_done if c == 0 else C.cnt["ve"])
                    a_abs = 0
                    for pl in range(4):
                        a_abs = C.inc(ACT.activation(
                            aux[:, pl * 512:(pl + 1) * 512], tiles[pl], AF.Abs), "ac")
                    C.wait(VEC, "ac", a_abs)
                    prs = {}
                    pairs = [(0, 1), (0, 2), (0, 3), (1, 2), (1, 3), (2, 3)]
                    vv = 0
                    for k, (pA, pB) in enumerate(pairs):
                        vv = C.inc(VEC.tensor_tensor(
                            aux[:, 2048 + k * 512: 2048 + (k + 1) * 512],
                            aux[:, pA * 512:(pA + 1) * 512],
                            aux[:, pB * 512:(pB + 1) * 512], ALU.is_ge), "ve")
                        prs[(pA, pB)] = aux[:, 2048 + k * 512: 2048 + (k + 1) * 512]
                    X1 = aux[:, 5120:5632]
                    X2 = aux[:, 5632:6144]
                    C.wait(VEC, "ve", vv)
                    v = C.inc(VEC.tensor_add(X1, prs[(0, 1)], prs[(0, 2)]), "ve")
                    C.wait(VEC, "ve", v)
                    v = C.inc(VEC.scalar_tensor_tensor(X2, prs[(0, 3)], 0.0, X1,
                                                       op0=ALU.add, op1=ALU.add), "ve")
                    C.wait(VEC, "ve", v)
                    C.inc(VEC.scalar_tensor_tensor(tiles[0], X2, 2.0, tiles[0],
                                                   op0=ALU.is_ge, op1=ALU.mult), "ve")
                    v = C.inc(VEC.tensor_add(X1, prs[(1, 2)], prs[(1, 3)]), "ve")
                    C.wait(VEC, "ve", v)
                    v = C.inc(VEC.scalar_tensor_tensor(X2, X1, 0.0, prs[(0, 1)],
                                                       op0=ALU.add, op1=ALU.subtract), "ve")
                    C.wait(VEC, "ve", v)
                    C.inc(VEC.scalar_tensor_tensor(tiles[1], X2, 1.0, tiles[1],
                                                   op0=ALU.is_ge, op1=ALU.mult), "ve")
                    v = C.inc(VEC.tensor_add(X1, prs[(0, 2)], prs[(1, 2)]), "ve")
                    C.wait(VEC, "ve", v)
                    v = C.inc(VEC.scalar_tensor_tensor(X2, prs[(2, 3)], 0.0, X1,
                                                       op0=ALU.add, op1=ALU.subtract), "ve")
                    C.wait(VEC, "ve", v)
                    C.inc(VEC.scalar_tensor_tensor(tiles[2], X2, 0.0, tiles[2],
                                                   op0=ALU.is_ge, op1=ALU.mult), "ve")
                    v = C.inc(VEC.tensor_add(X1, prs[(0, 3)], prs[(1, 3)]), "ve")
                    C.wait(VEC, "ve", v)
                    v = C.inc(VEC.scalar_tensor_tensor(X2, X1, 0.0, prs[(2, 3)],
                                                       op0=ALU.add, op1=ALU.add), "ve")
                    C.wait(VEC, "ve", v)
                    C.inc(VEC.scalar_tensor_tensor(tiles[3], X2, 1.0, tiles[3],
                                                   op0=ALU.is_le, op1=ALU.mult), "ve")
                    sp_done_c.append(C.cnt["ve"])
                C.rel("auxtile", "ac", a_abs)

                def load_wd(ec):
                    sl = ec % 2
                    C.use(SYN, f"wd{sl}")
                    if sl == 0:
                        for r in ("wg0", "wg1", "wu0", "wu1"):
                            C.use(SYN, r)
                    return ddma(SYN, "qd", wbuf[:, sl * 8192:(sl + 1) * 8192],
                                wb["wd"][ec, :, :])

                wdarr = {0: load_wd(0)}
                f_done = 0
                for ec in range(16):
                    sl = ec % 2
                    if ec + 1 < 16:
                        wdarr[ec + 1] = load_wd(ec + 1)
                    bank = f"ps{sl}"
                    C.use(TEN, bank)
                    C.wait(TEN, "qd", wdarr[ec])
                    first = True
                    for c in range(16):
                        C.wait(TEN, "ve", sp_done_c[c])
                        for pl in range(4):
                            ic = c + 16 * pl
                            ins = TEN.matmul(
                                ps[sl][:, :],
                                wbuf[:, sl * 8192 + ic * 128: sl * 8192 + ic * 128 + 128],
                                ln1T[:, ic * 512:(ic + 1) * 512],
                                start=first, stop=(c == 15 and pl == 3))
                            first = False
                    p_f = C.inc(ins, "pe")
                    C.rel(f"wd{sl}", "pe", p_f)
                    C.wait(ACT, "pe", p_f)
                    C.use(ACT, "fT")
                    a_f = C.inc(ACT.copy(kv[:, 8192:8704], ps[sl][:, :]), "ac")
                    C.rel(bank, "ac", a_f)
                    C.use(TEN, "pt")
                    C.wait(TEN, "ac", a_f)
                    for j in range(4):
                        ins = TEN.transpose(pt[:, j * 128:(j + 1) * 128],
                                            kv[:, 8192 + j * 128: 8192 + (j + 1) * 128],
                                            ident[:, :])
                    p_t = C.inc(ins, "pe")
                    C.rel("fT", "pe", p_t)
                    C.wait(VEC, "pe", p_t)
                    for j in range(4):
                        f_done = C.inc(VEC.tensor_copy(
                            kv[:, j * 2048 + ec * 128: j * 2048 + ec * 128 + 128],
                            pt[:, j * 128:(j + 1) * 128]), "ve")
                    C.rel("pt", "ve", f_done)
                C.rel("hT", "pe", p_f)

                for j in range(4):
                    trow = tbase + j * 128
                    f_tile = kv[:, j * 2048:(j + 1) * 2048]
                    v = ln_tile(f_tile, xf[:, 4096:6144], xf[:, 4096:6144],
                                pre_wait=("ve", f_done))
                    vc = C.inc(VEC.tensor_copy(xf[:, 2048:4096],
                                               ybuf[:, j * 2048:(j + 1) * 2048]), "ve")
                    C.wait(VEC, "ve", vc)
                    C.use(VEC, "outstage")
                    vo = C.inc(VEC.tensor_add(xf[:, 0:2048], xf[:, 4096:6144],
                                              xf[:, 2048:4096]), "ve")
                    C.wait(ACT, "ve", vo)
                    st = ddma(ACT, "qs", out_d[b, trow:trow + 128, :], xf[:, 0:2048])
                    C.rel("outstage", "qs", st)
                C.rel("ybufR", "ve", vo)
        C.end_block()

    with nc.Block():
        SYN.wait_ge(C.sems["qs"], C.cnt["qs"])
    C.end_block()
    return nc


_NC_CACHE = {}


def kernel(L0_attn_hidden_states, L0_attn_Wq, L0_attn_Wk_down, L0_attn_Wv_down,
           L0_attn_q_to_latent, L0_attn_v_up, L0_attn_Wout,
           L0_ffn_w_gate, L0_ffn_w_up, L0_ffn_w_down):
    x = np.ascontiguousarray(np.asarray(L0_attn_hidden_states, dtype=np.float32))
    Wq = np.asarray(L0_attn_Wq, dtype=np.float32)
    Wk = np.asarray(L0_attn_Wk_down, dtype=np.float32)
    Wv = np.asarray(L0_attn_Wv_down, dtype=np.float32)
    q2l = np.asarray(L0_attn_q_to_latent, dtype=np.float32)
    vup = np.asarray(L0_attn_v_up, dtype=np.float32)
    Wout = np.asarray(L0_attn_Wout, dtype=np.float32)
    wg = np.asarray(L0_ffn_w_gate, dtype=np.float32)
    wu = np.asarray(L0_ffn_w_up, dtype=np.float32)
    wd = np.asarray(L0_ffn_w_down, dtype=np.float32)

    pidx = 4 * (np.arange(I) % 2048) + np.arange(I) // 2048
    wg_p = wg[:, pidx]
    wu_p = wu[:, pidx]
    wd_p = wd[pidx, :]

    wq_host = np.ascontiguousarray(
        Wq.reshape(16, 128, 16, 128).transpose(2, 1, 0, 3).reshape(16, 128, 2048))
    wk_host = np.ascontiguousarray(
        Wk.reshape(16, 128, 512).transpose(1, 0, 2).reshape(128, 8192))
    wv_host = np.ascontiguousarray(
        Wv.reshape(16, 128, 512).transpose(1, 0, 2).reshape(128, 8192))
    q2l_host = np.ascontiguousarray(q2l.reshape(16, 128, 512))
    vup_host = np.ascontiguousarray(
        vup.reshape(16, 4, 128, 128).transpose(0, 2, 1, 3).reshape(16, 128, 512))
    wout_host = np.ascontiguousarray(
        Wout.reshape(16, 128, 16, 128).transpose(2, 1, 0, 3).reshape(16, 128, 2048))
    wg_host = np.ascontiguousarray(
        wg_p.reshape(16, 128, 64, 128).transpose(2, 1, 0, 3).reshape(64, 128, 2048))
    wu_host = np.ascontiguousarray(
        wu_p.reshape(16, 128, 64, 128).transpose(2, 1, 0, 3).reshape(64, 128, 2048))
    wd_host = np.ascontiguousarray(
        wd_p.reshape(64, 128, 16, 128).transpose(2, 1, 0, 3).reshape(16, 128, 8192))

    ii = np.arange(128)[:, None]
    jj = np.arange(512)[None, :]
    masks = np.concatenate(
        [np.where(jj - 128 * m >= ii, 1.0, 0.0) for m in range(4)],
        axis=1).astype(ml_dtypes.bfloat16)
    ident = np.eye(128).astype(ml_dtypes.bfloat16)
    ones = np.ones((128, 128), dtype=ml_dtypes.bfloat16)

    if "nc" not in _NC_CACHE:
        _NC_CACHE["nc"] = build_nc()
    nc = _NC_CACHE["nc"]

    base = {
        "wq": wq_host, "wk": wk_host, "wv": wv_host, "q2l": q2l_host,
        "vup": vup_host, "wout": wout_host, "wg": wg_host, "wu": wu_host,
        "wd": wd_host, "masks": masks, "ident": ident, "ones": ones,
        "cvals": np.tile(np.array([[EPS, -0.5, SCALE, 0.0, 0.0, 0.0, 0.0, 0.0]],
                          dtype=np.float32), (128, 1)),
    }
    in_maps = []
    for c in range(8):
        m = dict(base)
        m["x"] = np.ascontiguousarray(x[2 * c: 2 * c + 2])
        in_maps.append(m)

    res = run_bass_kernel_spmd(nc, in_maps, core_ids=list(range(8)))
    out = np.concatenate([np.asarray(r["out"]) for r in res.results], axis=0)
    return out.astype(np.float32)


# version 09
# speedup vs baseline: 1.0563x; 1.0210x over previous
import numpy as np
import ml_dtypes
import concourse.bass as bass
import concourse.mybir as mybir
from concourse.bass_utils import run_bass_kernel_spmd

F32 = mybir.dt.float32
BF16 = mybir.dt.bfloat16
AF = mybir.ActivationFunctionType
ALU = mybir.AluOpType
AX = mybir.AxisListType

E = 2048
T = 2048
H = 16
LAT = 512
I = 8192
NB = 2
SCALE = LAT ** -0.5
EPS = 1e-5
NEG = -1e9

COMPUTE_SEMS = ("pe", "ac", "ve")


class Ctl:

    def __init__(self, nc, engines):
        self.nc = nc
        self.engines = engines
        self.sems = {}
        self.cnt = {}
        self.waited = {}
        self.free = {}

    def sem(self, name):
        self.sems[name] = self.nc.alloc_semaphore(name)
        self.cnt[name] = 0

    def inc(self, instr, s, n=1):
        instr.then_inc(self.sems[s], n)
        self.cnt[s] += n
        return self.cnt[s]

    def wait(self, eng, s, v):
        if v <= 0:
            return
        assert v <= self.cnt[s], (s, v, self.cnt[s])
        k = (id(eng), s)
        if self.waited.get(k, 0) >= v:
            return
        eng.wait_ge(self.sems[s], v)
        self.waited[k] = v

    def use(self, eng, res):
        cond = self.free.get(res)
        if cond is not None:
            self.wait(eng, cond[0], cond[1])

    def rel(self, res, s, v):
        self.free[res] = (s, v)

    def end_block(self):
        for eng in self.engines:
            for s in COMPUTE_SEMS:
                self.waited[(id(eng), s)] = self.cnt[s]
        self.free = {}


def build_nc():
    nc = bass.Bass("TRN2", target_bir_lowering=False)
    TEN, VEC, ACT, SYN = nc.tensor, nc.vector, nc.scalar, nc.sync

    x_d = nc.dram_tensor("x", [NB, T, E], F32, kind="ExternalInput")
    out_d = nc.dram_tensor("out", [NB, T, E], F32, kind="ExternalOutput")
    wf, wb = {}, {}
    wshapes = {
        "wq": [16, 128, 2048], "wk": [128, 8192], "wv": [128, 8192],
        "q2l": [16, 128, 512], "vup": [16, 128, 512], "wout": [16, 128, 2048],
        "wg": [64, 128, 2048], "wu": [64, 128, 2048], "wd": [16, 128, 8192],
    }
    for n, sh in wshapes.items():
        wf[n] = nc.dram_tensor(n, sh, F32, kind="ExternalInput")
        wb[n] = nc.dram_tensor(n + "b", sh, BF16, kind="Internal")
    masks_d = nc.dram_tensor("masks", [128, 2048], BF16, kind="ExternalInput")
    ident_d = nc.dram_tensor("ident", [128, 128], BF16, kind="ExternalInput")
    ones_d = nc.dram_tensor("ones", [128, 128], BF16, kind="ExternalInput")
    cvals_d = nc.dram_tensor("cvals", [128, 8], F32, kind="ExternalInput")
    yscr = nc.dram_tensor("yscr", [NB, T, E], BF16, kind="Internal")

    ln1T = nc.alloc_sbuf_tensor("ln1T", [128, 32768], BF16)
    OT = nc.alloc_sbuf_tensor("OT", [128, 8192], BF16)
    kv = nc.alloc_sbuf_tensor("kv", [128, 16384], BF16)
    wbuf = nc.alloc_sbuf_tensor("wbuf", [128, 16384], BF16)
    ybuf = nc.alloc_sbuf_tensor("ybuf", [128, 8192], BF16)
    aux = nc.alloc_sbuf_tensor("aux", [128, 8192], BF16)
    xf = nc.alloc_sbuf_tensor("xf", [128, 6144], F32)
    consts = nc.alloc_sbuf_tensor("cmask", [128, 2048], BF16)
    ident = nc.alloc_sbuf_tensor("ident_sb", [128, 128], BF16)
    ones = nc.alloc_sbuf_tensor("ones_sb", [128, 128], BF16)
    stats = nc.alloc_sbuf_tensor("stats", [128, 16], F32)
    rzf = nc.alloc_sbuf_tensor("rzf", [1, 512], F32)
    rzb = nc.alloc_sbuf_tensor("rzb", [1, 512], BF16)
    cvals = nc.alloc_sbuf_tensor("cvals_sb", [128, 8], F32)
    nc.const_aps.register(F32, EPS, cvals[:, 0:1])
    nc.const_aps.register(F32, -0.5, cvals[:, 1:2])
    nc.const_aps.register(F32, SCALE, cvals[:, 2:3])

    pa = [nc.alloc_psum_tensor(f"pa{i}", [128, 512], F32) for i in range(4)]
    ps = [nc.alloc_psum_tensor(f"ps{i}", [128, 512], F32) for i in range(2)]
    pz = nc.alloc_psum_tensor("pz", [128, 512], F32)
    pt = nc.alloc_psum_tensor("pt", [128, 1024], BF16)

    engines = [TEN, VEC, ACT, SYN]
    C = Ctl(nc, engines)
    for s in ("qd", "qs", "pe", "ac", "ve"):
        C.sem(s)

    def ddma(eng, qsem, out, in_):
        return C.inc(eng.dma_start(out=out, in_=in_), qsem, 16)

    def qs_guard():
        for eng in (SYN, ACT, VEC):
            C.wait(eng, "qs", C.cnt["qs"])

    def ln_tile(in_tile, sqdump, out_ap, pre_wait=None, out_war=None):
        if pre_wait is not None:
            C.wait(VEC, pre_wait[0], pre_wait[1])
        C.inc(VEC.scalar_tensor_tensor(
            out=sqdump, in0=in_tile, scalar=1.0, in1=in_tile,
            op0=ALU.mult, op1=ALU.mult, accum_out=stats[:, 1:2]), "ve")
        v = C.inc(VEC.reduce_sum(stats[:, 0:1], in_tile, axis=AX.X), "ve")
        C.wait(VEC, "ve", v)
        v = C.inc(VEC.tensor_scalar_mul(stats[:, 2:3], stats[:, 0:1], 1.0 / E), "ve")
        C.wait(VEC, "ve", v)
        v = C.inc(VEC.tensor_mul(stats[:, 3:4], stats[:, 2:3], stats[:, 2:3]), "ve")
        C.wait(VEC, "ve", v)
        v = C.inc(VEC.scalar_tensor_tensor(
            out=stats[:, 4:5], in0=stats[:, 1:2], scalar=1.0 / E, in1=stats[:, 3:4],
            op0=ALU.mult, op1=ALU.subtract), "ve")
        C.wait(ACT, "ve", v)
        a = C.inc(ACT.activation(stats[:, 5:6], stats[:, 4:5], AF.Ln, bias=EPS), "ac")
        C.wait(ACT, "ac", a)
        a = C.inc(ACT.activation(stats[:, 6:7], stats[:, 5:6], AF.Exp, scale=-0.5), "ac")
        C.wait(VEC, "ac", a)
        if out_war is not None:
            C.use(VEC, out_war)
        v = C.inc(VEC.tensor_scalar(
            out=out_ap, in0=in_tile, scalar1=stats[:, 2:3], scalar2=stats[:, 6:7],
            op0=ALU.subtract, op1=ALU.mult), "ve")
        return v

    def transpose_tile(src_col_fn, dest_sb, dest_col_fn, src_gate):
        last_v = None
        for g in range(4):
            C.use(TEN, "pt")
            C.wait(TEN, src_gate[0], src_gate[1])
            for j in range(4):
                ins = TEN.transpose(pt[:, j * 128:(j + 1) * 128],
                                    src_col_fn(g * 4 + j), ident[:, :])
            p = C.inc(ins, "pe")
            C.wait(VEC, "pe", p)
            for j in range(4):
                dc = dest_col_fn(g * 4 + j)
                last_v = C.inc(VEC.tensor_copy(
                    dest_sb[:, dc:dc + 128], pt[:, j * 128:(j + 1) * 128]), "ve")
            C.rel("pt", "ve", last_v)
        return last_v, C.cnt["pe"]

    with nc.Block():
        ddma(SYN, "qd", consts[:, :], masks_d[:, :])
        ddma(SYN, "qd", ident[:, :], ident_d[:, :])
        ddma(SYN, "qd", ones[:, :], ones_d[:, :])
        ddma(SYN, "qd", cvals[:, :], cvals_d[:, :])

        xin = xf[:, 0:4096]
        bout = [aux[:, 0:4096], aux[:, 4096:8192]]
        ci = 0
        for n, sh in wshapes.items():
            nel = int(np.prod(sh))
            nch = nel // (128 * 4096)
            if len(sh) == 3:
                src = wf[n][:, :, :].rearrange("a b c -> (a b c)")
                dst = wb[n][:, :, :].rearrange("a b c -> (a b c)")
            else:
                src = wf[n][:, :].rearrange("a b -> (a b)")
                dst = wb[n][:, :].rearrange("a b -> (a b)")
            src = src.rearrange("(n p w) -> n p w", p=128, w=4096)
            dst = dst.rearrange("(n p w) -> n p w", p=128, w=4096)
            for i in range(nch):
                sl = ci % 2
                C.use(SYN, "p0in")
                ld = ddma(SYN, "qd", xin, src[i, :, :])
                eng, esem = (ACT, "ac") if sl == 0 else (VEC, "ve")
                C.wait(eng, "qd", ld)
                C.use(eng, f"p0out{sl}")
                if sl == 0:
                    cv = C.inc(ACT.copy(bout[sl], xin), "ac")
                else:
                    cv = C.inc(VEC.tensor_copy(bout[sl], xin), "ve")
                C.rel("p0in", esem, cv)
                C.wait(ACT, esem, cv)
                st = ddma(ACT, "qs", dst[i, :, :], bout[sl])
                C.rel(f"p0out{sl}", "qs", st)
                ci += 1
        SYN.wait_ge(C.sems["qs"], C.cnt["qs"])
        SYN.wait_ge(C.sems["qd"], C.cnt["qd"])
        C.waited[(id(SYN), "qs")] = C.cnt["qs"]
        C.waited[(id(SYN), "qd")] = C.cnt["qd"]
    C.end_block()

    for b in range(NB):
        with nc.Block():
            qs_guard()
            lwk = ddma(ACT, "qs", wbuf[:, 0:8192], wb["wk"][:, :])
            lwv = ddma(ACT, "qs", wbuf[:, 8192:16384], wb["wv"][:, :])
            lv = {}
            rot = 0
            for tt in range(16):
                C.use(SYN, "xf0")
                ld = ddma(SYN, "qd", xf[:, 0:2048], x_d[b, tt * 128:(tt + 1) * 128, :])
                C.wait(VEC, "qd", ld)
                v = ln_tile(xf[:, 0:2048], xf[:, 4096:6144], aux[:, 0:2048],
                            out_war="auxtile")
                C.rel("xf0", "ve", v)
                lv[tt], pe_src = transpose_tile(
                    lambda ec: aux[:, ec * 128:(ec + 1) * 128],
                    ln1T, lambda ec: ec * 2048 + tt * 128, ("ve", v))
                C.rel("auxtile", "pe", pe_src)
                bank = f"pa{rot % 4}"
                C.use(TEN, bank)
                C.wait(TEN, "qs", lwv)
                C.wait(TEN, "ve", lv[tt])
                for ec in range(16):
                    ins = TEN.matmul(
                        pa[rot % 4][:, :],
                        ln1T[:, ec * 2048 + tt * 128: ec * 2048 + tt * 128 + 128],
                        wbuf[:, 8192 + ec * 512: 8192 + (ec + 1) * 512],
                        start=(ec == 0), stop=(ec == 15))
                p = C.inc(ins, "pe")
                C.wait(ACT, "pe", p)
                a = C.inc(ACT.copy(kv[:, 8192 + tt * 512: 8192 + (tt + 1) * 512],
                                   pa[rot % 4][:, :]), "ac")
                C.rel(bank, "ac", a)
                rot += 1
                if tt % 4 == 3:
                    tc = tt // 4
                    for lc in range(4):
                        bank = f"pa{rot % 4}"
                        C.use(TEN, bank)
                        for ec in range(16):
                            ins = TEN.matmul(
                                pa[rot % 4][:, :],
                                wbuf[:, ec * 512 + lc * 128: ec * 512 + lc * 128 + 128],
                                ln1T[:, ec * 2048 + tc * 512: ec * 2048 + (tc + 1) * 512],
                                start=(ec == 0), stop=(ec == 15))
                        p = C.inc(ins, "pe")
                        C.wait(ACT, "pe", p)
                        a = C.inc(ACT.copy(
                            kv[:, lc * 2048 + tc * 512: lc * 2048 + (tc + 1) * 512],
                            pa[rot % 4][:, :]), "ac")
                        C.rel(bank, "ac", a)
                        rot += 1
        C.end_block()

        with nc.Block():
            qs_guard()
            for qc in range(4):
                nkt = 4 * qc + 4
                C.wait(ACT, "qs", C.cnt["qs"])
                C.wait(VEC, "qs", C.cnt["qs"])

                def load_head(h):
                    sl = h % 2
                    C.use(SYN, f"wq{sl}")
                    ddma(SYN, "qd", wbuf[:, sl * 2048:(sl + 1) * 2048], wb["wq"][h, :, :])
                    C.use(SYN, f"q2l{sl}")
                    ddma(SYN, "qd", wbuf[:, 4096 + sl * 512: 4096 + (sl + 1) * 512],
                         wb["q2l"][h, :, :])
                    C.use(SYN, f"vup{sl}")
                    return ddma(SYN, "qd", wbuf[:, 5120 + sl * 512: 5120 + (sl + 1) * 512],
                                wb["vup"][h, :, :])

                warr = {0: load_head(0)}

                def emit_qh(h):
                    sl = h % 2
                    C.wait(TEN, "qd", warr[h])
                    C.use(TEN, "ps0")
                    for ec in range(16):
                        ins = TEN.matmul(
                            ps[0][:, :],
                            wbuf[:, sl * 2048 + ec * 128: sl * 2048 + ec * 128 + 128],
                            ln1T[:, ec * 2048 + qc * 512: ec * 2048 + (qc + 1) * 512],
                            start=(ec == 0), stop=(ec == 15))
                    p = C.inc(ins, "pe")
                    C.rel(f"wq{sl}", "pe", p)
                    return p

                p_qh = {0: emit_qh(0)}
                a_o = 0
                for h in range(H):
                    sl = h % 2
                    if h + 1 < H:
                        warr[h + 1] = load_head(h + 1)
                    q2l_sl = wbuf[:, 4096 + sl * 512: 4096 + (sl + 1) * 512]
                    vup_sl = wbuf[:, 5120 + sl * 512: 5120 + (sl + 1) * 512]
                    C.wait(ACT, "pe", p_qh[h])
                    C.use(ACT, "qh")
                    a_qh = C.inc(ACT.copy(ybuf[:, 0:512], ps[0][:, :]), "ac")
                    C.rel("ps0", "ac", a_qh)
                    a_ql = 0
                    for lc in range(4):
                        C.use(TEN, f"pa{lc}")
                        C.wait(TEN, "ac", a_qh)
                        pm = C.inc(TEN.matmul(
                            pa[lc][:, :], q2l_sl[:, lc * 128:(lc + 1) * 128],
                            ybuf[:, 0:512], start=True, stop=True), "pe")
                        if lc == 3:
                            C.rel("qh", "pe", pm)
                        C.wait(ACT, "pe", pm)
                        C.use(ACT, "qlat")
                        a_ql = C.inc(ACT.activation(
                            ybuf[:, 512 + lc * 512: 512 + (lc + 1) * 512],
                            pa[lc][:, :], AF.Copy, scale=SCALE), "ac")
                        C.rel(f"pa{lc}", "ac", a_ql)
                    p_ctx_last = {}
                    p_z = 0

                    def emit_S(kt):
                        sb = kt % 2
                        C.use(TEN, f"ps{sb}")
                        C.wait(TEN, "ac", a_ql)
                        for lc in range(4):
                            ins = TEN.matmul(
                                ps[sb][:, :],
                                kv[:, lc * 2048 + kt * 128: lc * 2048 + kt * 128 + 128],
                                ybuf[:, 512 + lc * 512: 512 + (lc + 1) * 512],
                                start=(lc == 0), stop=(lc == 3))
                        return C.inc(ins, "pe")

                    p_s = emit_S(0)
                    for kt in range(nkt):
                        sb = kt % 2
                        C.wait(ACT, "pe", p_s)
                        C.use(ACT, f"E{sb}")
                        a_e = C.inc(ACT.activation(
                            ybuf[:, 2560 + sb * 512: 2560 + (sb + 1) * 512],
                            ps[sb][:, :], AF.Exp), "ac")
                        C.rel(f"ps{sb}", "ac", a_e)
                        e_gate = ("ac", a_e)
                        C.wait(VEC, "ac", a_e)
                        if kt >= 4 * qc:
                            m = kt - 4 * qc
                            vm = C.inc(VEC.tensor_tensor(
                                ybuf[:, 2560 + sb * 512: 2560 + (sb + 1) * 512],
                                ybuf[:, 2560 + sb * 512: 2560 + (sb + 1) * 512],
                                consts[:, m * 512:(m + 1) * 512], ALU.mult), "ve")
                            e_gate = ("ve", vm)
                        if kt == 0:
                            C.inc(VEC.tensor_copy(
                                xf[:, 2048:2560],
                                ybuf[:, 2560 + sb * 512: 2560 + (sb + 1) * 512]), "ve")
                        else:
                            C.inc(VEC.tensor_add(
                                xf[:, 2048:2560], xf[:, 2048:2560],
                                ybuf[:, 2560 + sb * 512: 2560 + (sb + 1) * 512]), "ve")
                        if kt + 1 < nkt:
                            p_s = emit_S(kt + 1)
                        C.wait(TEN, e_gate[0], e_gate[1])
                        for lc in range(4):
                            if kt == 0:
                                C.use(TEN, f"pa{lc}")
                            ins = TEN.matmul(
                                pa[lc][:, :],
                                kv[:, 8192 + kt * 512 + lc * 128: 8192 + kt * 512 + lc * 128 + 128],
                                ybuf[:, 2560 + sb * 512: 2560 + (sb + 1) * 512],
                                start=(kt == 0), stop=(kt == nkt - 1))
                            if kt == nkt - 1:
                                p_ctx_last[lc] = C.inc(ins, "pe")
                        pm = C.cnt["pe"] if kt == nkt - 1 else C.inc(ins, "pe")
                        C.rel(f"E{sb}", "pe", pm)
                    C.use(VEC, "E0")
                    v_es = C.inc(VEC.tensor_copy(ybuf[:, 2560:3072], xf[:, 2048:2560]), "ve")
                    C.rel("E0", "ve", v_es)
                    if h + 1 < H:
                        p_qh[h + 1] = emit_qh(h + 1)
                    C.use(TEN, "pz")
                    C.wait(TEN, "ve", v_es)
                    p_z = C.inc(TEN.matmul(pz[0:1, :], ones[:, 0:1], ybuf[:, 2560:3072],
                                           start=True, stop=True), "pe")
                    C.wait(VEC, "pe", p_z)
                    v = C.inc(VEC.reciprocal(rzf[0:1, :], pz[0:1, :]), "ve")
                    C.wait(VEC, "ve", v)
                    v_rz = C.inc(VEC.tensor_copy(rzb[0:1, :], rzf[0:1, :]), "ve")
                    C.wait(TEN, "ve", v_rz)
                    p_bc = C.inc(TEN.matmul(pz[:, :], ones[0:1, 0:128], rzb[0:1, :],
                                            start=True, stop=True), "pe")
                    C.wait(ACT, "pe", p_bc)
                    C.use(ACT, "bcast")
                    a_bc = C.inc(ACT.copy(ybuf[:, 3584:4096], pz[:, :]), "ac")
                    C.rel("pz", "ac", a_bc)
                    v_ctx = 0
                    for lc in range(4):
                        C.wait(VEC, "ac", a_bc)
                        C.wait(VEC, "pe", p_ctx_last[lc])
                        C.use(VEC, "ctx")
                        v_ctx = C.inc(VEC.tensor_tensor(
                            ybuf[:, 4096 + lc * 512: 4096 + (lc + 1) * 512],
                            pa[lc][:, :], ybuf[:, 3584:4096], ALU.mult), "ve")
                        C.rel(f"pa{lc}", "ve", v_ctx)
                    C.use(TEN, "ps1")
                    C.wait(TEN, "ve", v_ctx)
                    for lc in range(4):
                        ins = TEN.matmul(
                            ps[1][:, :], vup_sl[:, lc * 128:(lc + 1) * 128],
                            ybuf[:, 4096 + lc * 512: 4096 + (lc + 1) * 512],
                            start=(lc == 0), stop=(lc == 3))
                    p_o = C.inc(ins, "pe")
                    C.rel(f"q2l{sl}", "pe", p_o)
                    C.rel(f"vup{sl}", "pe", p_o)
                    C.rel("ctx", "pe", p_o)
                    C.rel("bcast", "pe", p_o)
                    C.rel("qlat", "pe", p_o)
                    C.wait(ACT, "pe", p_o)
                    C.use(ACT, "OT")
                    a_o = C.inc(ACT.copy(OT[:, h * 512:(h + 1) * 512], ps[1][:, :]), "ac")
                    C.rel("ps1", "ac", a_o)
                a_o_all = a_o

                def load_wout(ec):
                    sl = ec % 2
                    C.use(SYN, f"wq{sl}")
                    return ddma(SYN, "qd", wbuf[:, sl * 2048:(sl + 1) * 2048],
                                wb["wout"][ec, :, :])

                woarr = {0: load_wout(0)}
                a_ao_arr = {}
                cell = {"ao_done": 0}

                def emit_ao_tr(ec):
                    sl = ec % 2
                    C.use(TEN, "pt")
                    C.wait(TEN, "ac", a_ao_arr[ec])
                    for j in range(4):
                        ins = TEN.transpose(
                            pt[:, j * 128:(j + 1) * 128],
                            ybuf[:, sl * 512 + j * 128: sl * 512 + (j + 1) * 128],
                            ident[:, :])
                    p_t = C.inc(ins, "pe")
                    C.rel(f"aoT{sl}", "pe", p_t)
                    C.wait(VEC, "pe", p_t)
                    for j in range(4):
                        cell["ao_done"] = C.inc(VEC.tensor_copy(
                            aux[:, j * 2048 + ec * 128: j * 2048 + ec * 128 + 128],
                            pt[:, j * 128:(j + 1) * 128]), "ve")
                    C.rel("pt", "ve", cell["ao_done"])

                for ec in range(16):
                    sl = ec % 2
                    if ec + 1 < 16:
                        woarr[ec + 1] = load_wout(ec + 1)
                    bank = f"ps{sl}"
                    C.use(TEN, bank)
                    C.wait(TEN, "qd", woarr[ec])
                    C.wait(TEN, "ac", a_o_all)
                    for hc in range(16):
                        ins = TEN.matmul(
                            ps[sl][:, :],
                            wbuf[:, sl * 2048 + hc * 128: sl * 2048 + hc * 128 + 128],
                            OT[:, hc * 512:(hc + 1) * 512],
                            start=(hc == 0), stop=(hc == 15))
                    p_ao = C.inc(ins, "pe")
                    C.rel(f"wq{sl}", "pe", p_ao)
                    C.wait(ACT, "pe", p_ao)
                    C.use(ACT, f"aoT{sl}")
                    a_ao_arr[ec] = C.inc(ACT.copy(ybuf[:, sl * 512:(sl + 1) * 512],
                                                  ps[sl][:, :]), "ac")
                    C.rel(bank, "ac", a_ao_arr[ec])
                    if ec > 0:
                        emit_ao_tr(ec - 1)
                emit_ao_tr(15)
                ao_done = cell["ao_done"]
                C.rel("OT", "pe", p_ao)

                for j in range(4):
                    trow = qc * 512 + j * 128
                    C.use(SYN, "xf0")
                    ld = ddma(SYN, "qd", xf[:, 0:2048], x_d[b, trow:trow + 128, :])
                    ao_tile = aux[:, j * 2048:(j + 1) * 2048]
                    v = ln_tile(ao_tile, xf[:, 4096:6144], xf[:, 4096:6144],
                                pre_wait=("ve", ao_done))
                    C.wait(VEC, "qd", ld)
                    C.use(VEC, "ytile")
                    vy = C.inc(VEC.tensor_add(ybuf[:, 2048:4096],
                                              xf[:, 4096:6144], xf[:, 0:2048]), "ve")
                    C.rel("xf0", "ve", vy)
                    C.wait(ACT, "ve", vy)
                    st = ddma(ACT, "qs", yscr[b, trow:trow + 128, :], ybuf[:, 2048:4096])
                    C.rel("ytile", "qs", st)
        C.end_block()

        with nc.Block():
            qs_guard()
            for s in range(4):
                tbase = s * 512
                C.wait(SYN, "qs", C.cnt["qs"])
                ln2_done = 0
                for j in range(4):
                    C.use(SYN, "ybufR")
                    ld = ddma(SYN, "qd", ybuf[:, j * 2048:(j + 1) * 2048],
                              yscr[b, tbase + j * 128: tbase + j * 128 + 128, :])
                    C.wait(VEC, "qd", ld)
                    v = ln_tile(ybuf[:, j * 2048:(j + 1) * 2048],
                                xf[:, 4096:6144], aux[:, 0:2048], out_war="auxtile")
                    C.use(VEC, "OT")
                    lv, pe_src = transpose_tile(
                        lambda ec: aux[:, ec * 128:(ec + 1) * 128],
                        OT, lambda ec: ec * 512 + j * 128, ("ve", v))
                    C.rel("auxtile", "pe", pe_src)
                    ln2_done = lv

                def load_gu(ic):
                    sl = ic % 2
                    C.use(SYN, "wd0")
                    C.use(SYN, f"wg{sl}")
                    g = ddma(SYN, "qd", wbuf[:, sl * 2048:(sl + 1) * 2048],
                             wb["wg"][ic, :, :])
                    C.use(ACT, "wd0")
                    C.use(ACT, f"wu{sl}")
                    u = ddma(ACT, "qs", wbuf[:, 4096 + sl * 2048: 4096 + (sl + 1) * 2048],
                             wb["wu"][ic, :, :])
                    return (g, u)

                guarr = {0: load_gu(0)}
                h_done = 0
                for ic in range(64):
                    sl = ic % 2
                    if ic + 1 < 64:
                        guarr[ic + 1] = load_gu(ic + 1)
                    gb, ub = 2 * sl, 2 * sl + 1
                    C.wait(TEN, "qd", guarr[ic][0])
                    C.wait(TEN, "qs", guarr[ic][1])
                    C.wait(TEN, "ve", ln2_done)
                    C.use(TEN, f"pa{gb}")
                    for ec in range(16):
                        ins = TEN.matmul(
                            pa[gb][:, :],
                            wbuf[:, sl * 2048 + ec * 128: sl * 2048 + ec * 128 + 128],
                            OT[:, ec * 512:(ec + 1) * 512],
                            start=(ec == 0), stop=(ec == 15))
                    p_g = C.inc(ins, "pe")
                    C.use(TEN, f"pa{ub}")
                    for ec in range(16):
                        ins = TEN.matmul(
                            pa[ub][:, :],
                            wbuf[:, 4096 + sl * 2048 + ec * 128: 4096 + sl * 2048 + ec * 128 + 128],
                            OT[:, ec * 512:(ec + 1) * 512],
                            start=(ec == 0), stop=(ec == 15))
                    p_u = C.inc(ins, "pe")
                    C.rel(f"wg{sl}", "pe", p_g)
                    C.rel(f"wu{sl}", "pe", p_u)
                    C.wait(ACT, "pe", p_g)
                    C.use(ACT, "silu")
                    a_si = C.inc(ACT.activation(aux[:, 2048:2560], pa[gb][:, :], AF.Silu), "ac")
                    C.rel(f"pa{gb}", "ac", a_si)
                    C.wait(VEC, "ac", a_si)
                    C.wait(VEC, "pe", p_u)
                    C.use(VEC, "hT")
                    h_done = C.inc(VEC.tensor_tensor(
                        ln1T[:, ic * 512:(ic + 1) * 512],
                        aux[:, 2048:2560], pa[ub][:, :], ALU.mult), "ve")
                    C.rel(f"pa{ub}", "ve", h_done)
                    C.rel("silu", "ve", h_done)
                C.rel("OT", "pe", p_u)

                sp_done_c = []
                for c in range(16):
                    tiles = [ln1T[:, (c + 16 * pl) * 512:(c + 16 * pl + 1) * 512]
                             for pl in range(4)]
                    C.wait(ACT, "ve", h_done if c == 0 else C.cnt["ve"])
                    a_abs = 0
                    for pl in range(4):
                        a_abs = C.inc(ACT.activation(
                            aux[:, pl * 512:(pl + 1) * 512], tiles[pl], AF.Abs), "ac")
                    C.wait(VEC, "ac", a_abs)
                    prs = {}
                    pairs = [(0, 1), (0, 2), (0, 3), (1, 2), (1, 3), (2, 3)]
                    vv = 0
                    for k, (pA, pB) in enumerate(pairs):
                        vv = C.inc(VEC.tensor_tensor(
                            aux[:, 2048 + k * 512: 2048 + (k + 1) * 512],
                            aux[:, pA * 512:(pA + 1) * 512],
                            aux[:, pB * 512:(pB + 1) * 512], ALU.is_ge), "ve")
                        prs[(pA, pB)] = aux[:, 2048 + k * 512: 2048 + (k + 1) * 512]
                    X1 = aux[:, 5120:5632]
                    X2 = aux[:, 5632:6144]
                    C.wait(VEC, "ve", vv)
                    v = C.inc(VEC.tensor_add(X1, prs[(0, 1)], prs[(0, 2)]), "ve")
                    C.wait(VEC, "ve", v)
                    v = C.inc(VEC.scalar_tensor_tensor(X2, prs[(0, 3)], 0.0, X1,
                                                       op0=ALU.add, op1=ALU.add), "ve")
                    C.wait(VEC, "ve", v)
                    C.inc(VEC.scalar_tensor_tensor(tiles[0], X2, 2.0, tiles[0],
                                                   op0=ALU.is_ge, op1=ALU.mult), "ve")
                    v = C.inc(VEC.tensor_add(X1, prs[(1, 2)], prs[(1, 3)]), "ve")
                    C.wait(VEC, "ve", v)
                    v = C.inc(VEC.scalar_tensor_tensor(X2, X1, 0.0, prs[(0, 1)],
                                                       op0=ALU.add, op1=ALU.subtract), "ve")
                    C.wait(VEC, "ve", v)
                    C.inc(VEC.scalar_tensor_tensor(tiles[1], X2, 1.0, tiles[1],
                                                   op0=ALU.is_ge, op1=ALU.mult), "ve")
                    v = C.inc(VEC.tensor_add(X1, prs[(0, 2)], prs[(1, 2)]), "ve")
                    C.wait(VEC, "ve", v)
                    v = C.inc(VEC.scalar_tensor_tensor(X2, prs[(2, 3)], 0.0, X1,
                                                       op0=ALU.add, op1=ALU.subtract), "ve")
                    C.wait(VEC, "ve", v)
                    C.inc(VEC.scalar_tensor_tensor(tiles[2], X2, 0.0, tiles[2],
                                                   op0=ALU.is_ge, op1=ALU.mult), "ve")
                    v = C.inc(VEC.tensor_add(X1, prs[(0, 3)], prs[(1, 3)]), "ve")
                    C.wait(VEC, "ve", v)
                    v = C.inc(VEC.scalar_tensor_tensor(X2, X1, 0.0, prs[(2, 3)],
                                                       op0=ALU.add, op1=ALU.add), "ve")
                    C.wait(VEC, "ve", v)
                    C.inc(VEC.scalar_tensor_tensor(tiles[3], X2, 1.0, tiles[3],
                                                   op0=ALU.is_le, op1=ALU.mult), "ve")
                    sp_done_c.append(C.cnt["ve"])
                C.rel("auxtile", "ac", a_abs)

                def load_wd(ec):
                    sl = ec % 2
                    C.use(SYN, f"wd{sl}")
                    if sl == 0:
                        for r in ("wg0", "wg1", "wu0", "wu1"):
                            C.use(SYN, r)
                    return ddma(SYN, "qd", wbuf[:, sl * 8192:(sl + 1) * 8192],
                                wb["wd"][ec, :, :])

                wdarr = {0: load_wd(0)}
                a_f_arr = {}
                fcell = {"f_done": 0}

                def emit_f_tr(ec):
                    sl = ec % 2
                    C.use(TEN, "pt")
                    C.wait(TEN, "ac", a_f_arr[ec])
                    for j in range(4):
                        ins = TEN.transpose(
                            pt[:, j * 128:(j + 1) * 128],
                            kv[:, 8192 + sl * 512 + j * 128: 8192 + sl * 512 + (j + 1) * 128],
                            ident[:, :])
                    p_t = C.inc(ins, "pe")
                    C.rel(f"fT{sl}", "pe", p_t)
                    C.wait(VEC, "pe", p_t)
                    for j in range(4):
                        fcell["f_done"] = C.inc(VEC.tensor_copy(
                            kv[:, j * 2048 + ec * 128: j * 2048 + ec * 128 + 128],
                            pt[:, j * 128:(j + 1) * 128]), "ve")
                    C.rel("pt", "ve", fcell["f_done"])

                for ec in range(16):
                    sl = ec % 2
                    if ec + 1 < 16:
                        wdarr[ec + 1] = load_wd(ec + 1)
                    bank = f"ps{sl}"
                    C.use(TEN, bank)
                    C.wait(TEN, "qd", wdarr[ec])
                    first = True
                    for c in range(16):
                        C.wait(TEN, "ve", sp_done_c[c])
                        for pl in range(4):
                            ic = c + 16 * pl
                            ins = TEN.matmul(
                                ps[sl][:, :],
                                wbuf[:, sl * 8192 + ic * 128: sl * 8192 + ic * 128 + 128],
                                ln1T[:, ic * 512:(ic + 1) * 512],
                                start=first, stop=(c == 15 and pl == 3))
                            first = False
                    p_f = C.inc(ins, "pe")
                    C.rel(f"wd{sl}", "pe", p_f)
                    C.wait(ACT, "pe", p_f)
                    C.use(ACT, f"fT{sl}")
                    a_f_arr[ec] = C.inc(ACT.copy(kv[:, 8192 + sl * 512: 8192 + (sl + 1) * 512],
                                                 ps[sl][:, :]), "ac")
                    C.rel(bank, "ac", a_f_arr[ec])
                    if ec > 0:
                        emit_f_tr(ec - 1)
                emit_f_tr(15)
                f_done = fcell["f_done"]
                C.rel("hT", "pe", p_f)

                for j in range(4):
                    trow = tbase + j * 128
                    f_tile = kv[:, j * 2048:(j + 1) * 2048]
                    v = ln_tile(f_tile, xf[:, 4096:6144], xf[:, 4096:6144],
                                pre_wait=("ve", f_done))
                    vc = C.inc(VEC.tensor_copy(xf[:, 2048:4096],
                                               ybuf[:, j * 2048:(j + 1) * 2048]), "ve")
                    C.wait(VEC, "ve", vc)
                    C.use(VEC, "outstage")
                    vo = C.inc(VEC.tensor_add(xf[:, 0:2048], xf[:, 4096:6144],
                                              xf[:, 2048:4096]), "ve")
                    C.wait(ACT, "ve", vo)
                    st = ddma(ACT, "qs", out_d[b, trow:trow + 128, :], xf[:, 0:2048])
                    C.rel("outstage", "qs", st)
                C.rel("ybufR", "ve", vo)
        C.end_block()

    with nc.Block():
        SYN.wait_ge(C.sems["qs"], C.cnt["qs"])
    C.end_block()
    return nc


_NC_CACHE = {}


def kernel(L0_attn_hidden_states, L0_attn_Wq, L0_attn_Wk_down, L0_attn_Wv_down,
           L0_attn_q_to_latent, L0_attn_v_up, L0_attn_Wout,
           L0_ffn_w_gate, L0_ffn_w_up, L0_ffn_w_down):
    x = np.ascontiguousarray(np.asarray(L0_attn_hidden_states, dtype=np.float32))
    Wq = np.asarray(L0_attn_Wq, dtype=np.float32)
    Wk = np.asarray(L0_attn_Wk_down, dtype=np.float32)
    Wv = np.asarray(L0_attn_Wv_down, dtype=np.float32)
    q2l = np.asarray(L0_attn_q_to_latent, dtype=np.float32)
    vup = np.asarray(L0_attn_v_up, dtype=np.float32)
    Wout = np.asarray(L0_attn_Wout, dtype=np.float32)
    wg = np.asarray(L0_ffn_w_gate, dtype=np.float32)
    wu = np.asarray(L0_ffn_w_up, dtype=np.float32)
    wd = np.asarray(L0_ffn_w_down, dtype=np.float32)

    pidx = 4 * (np.arange(I) % 2048) + np.arange(I) // 2048
    wg_p = wg[:, pidx]
    wu_p = wu[:, pidx]
    wd_p = wd[pidx, :]

    wq_host = np.ascontiguousarray(
        Wq.reshape(16, 128, 16, 128).transpose(2, 1, 0, 3).reshape(16, 128, 2048))
    wk_host = np.ascontiguousarray(
        Wk.reshape(16, 128, 512).transpose(1, 0, 2).reshape(128, 8192))
    wv_host = np.ascontiguousarray(
        Wv.reshape(16, 128, 512).transpose(1, 0, 2).reshape(128, 8192))
    q2l_host = np.ascontiguousarray(q2l.reshape(16, 128, 512))
    vup_host = np.ascontiguousarray(
        vup.reshape(16, 4, 128, 128).transpose(0, 2, 1, 3).reshape(16, 128, 512))
    wout_host = np.ascontiguousarray(
        Wout.reshape(16, 128, 16, 128).transpose(2, 1, 0, 3).reshape(16, 128, 2048))
    wg_host = np.ascontiguousarray(
        wg_p.reshape(16, 128, 64, 128).transpose(2, 1, 0, 3).reshape(64, 128, 2048))
    wu_host = np.ascontiguousarray(
        wu_p.reshape(16, 128, 64, 128).transpose(2, 1, 0, 3).reshape(64, 128, 2048))
    wd_host = np.ascontiguousarray(
        wd_p.reshape(64, 128, 16, 128).transpose(2, 1, 0, 3).reshape(16, 128, 8192))

    ii = np.arange(128)[:, None]
    jj = np.arange(512)[None, :]
    masks = np.concatenate(
        [np.where(jj - 128 * m >= ii, 1.0, 0.0) for m in range(4)],
        axis=1).astype(ml_dtypes.bfloat16)
    ident = np.eye(128).astype(ml_dtypes.bfloat16)
    ones = np.ones((128, 128), dtype=ml_dtypes.bfloat16)

    if "nc" not in _NC_CACHE:
        _NC_CACHE["nc"] = build_nc()
    nc = _NC_CACHE["nc"]

    base = {
        "wq": wq_host, "wk": wk_host, "wv": wv_host, "q2l": q2l_host,
        "vup": vup_host, "wout": wout_host, "wg": wg_host, "wu": wu_host,
        "wd": wd_host, "masks": masks, "ident": ident, "ones": ones,
        "cvals": np.tile(np.array([[EPS, -0.5, SCALE, 0.0, 0.0, 0.0, 0.0, 0.0]],
                          dtype=np.float32), (128, 1)),
    }
    in_maps = []
    for c in range(8):
        m = dict(base)
        m["x"] = np.ascontiguousarray(x[2 * c: 2 * c + 2])
        in_maps.append(m)

    res = run_bass_kernel_spmd(nc, in_maps, core_ids=list(range(8)))
    out = np.concatenate([np.asarray(r["out"]) for r in res.results], axis=0)
    return out.astype(np.float32)
